# version 06
# baseline (speedup 1.0000x reference)
import math
import numpy as np
import contextlib
import concourse.bass as bass
import concourse.mybir as mybir
from concourse.bass_utils import run_bass_kernel_spmd

F32 = mybir.dt.float32
BF16 = mybir.dt.bfloat16
AF = mybir.ActivationFunctionType
ALU = mybir.AluOpType
AX = mybir.AxisListType.X

S, E, H, HD = 4096, 1024, 16, 64
HID, NE = 4096, 8
NBLK = S // 128
NC = E // 128
DS = 1.0 / math.sqrt(12.0)
NEG = -1.0e30
PAIRS = [(qm + g, qm, g) for g in range(8, 16) for qm in range(0, 16 - g)]
PAIR_IDX = {(hh, qm): i for i, (hh, qm, g) in enumerate(PAIRS)}

N_PHASES = 4


class P:
    def __init__(self, nc):
        self.nc = nc
        self.sem = {}
        self.cnt = {}
        self.ctx = contextlib.ExitStack()

    def mksem(self, *names):
        for n in names:
            self.sem[n] = self.ctx.enter_context(self.nc.semaphore(n))
            self.cnt[n] = 0

    def inc(self, instr, name, n=1):
        instr.then_inc(self.sem[name], n)
        self.cnt[name] += n
        return self.cnt[name]

    def bump(self, name, n):
        self.cnt[name] += n
        return self.cnt[name]

    def wait(self, eng, name, val):
        if val > 0:
            eng.wait_ge(self.sem[name], val)

    def check(self, name, val):
        assert self.cnt[name] == val, (name, self.cnt[name], val)


def ap(t, offset, dims):
    return bass.AP(tensor=t.tensor if hasattr(t, "tensor") else t, offset=offset,
                   ap=[list(d) for d in dims])


def ln_vec_pre(p, v, x_ap, mu, xc_out, mark):
    n = x_ap.shape[-1]
    c = p.inc(v.reduce_sum(mu, x_ap, axis=AX), "sVc")
    p.wait(v, "sVc", c)
    c = p.inc(v.tensor_scalar_mul(mu, mu, 1.0 / n), "sVc")
    p.wait(v, "sVc", c)
    v.tensor_scalar(xc_out, x_ap, mu, None, ALU.subtract).then_inc(p.sem[mark], 1)


def ln_scalar(p, sc, xc_ap, dump_ap, var, rs, eps_ap, mark, v1):
    n = xc_ap.shape[-1]
    c0 = p.inc(sc.activation(dump_ap, xc_ap, AF.Square, accum_out=var), "sSc")
    p.wait(sc, "sSc", c0)
    sc.activation(rs, var, AF.Ln, scale=1.0 / n, bias=eps_ap).then_inc(p.sem[mark], 1)
    p.wait(sc, mark, v1)
    sc.activation(rs, rs, AF.Exp, scale=-0.5).then_inc(p.sem[mark], 1)


def emit_s1(nc, p, src, dst, wq_in, wk_in, wv_in, tg_bf, gamma_bc_in, beta_bc_in,
            k1_hbm, idb, idf, onesb, epsc, cast1):
  with contextlib.ExitStack() as st:
    vaug = st.enter_context(nc.sbuf_tensor("s1_vaug", [128, NBLK * (H * 65)], BF16))
    q_sb = st.enter_context(nc.sbuf_tensor("s1_q", [128, NC * S], BF16))

    with contextlib.ExitStack() as s2, nc.Block() as block:
        wqs = s2.enter_context(nc.sbuf_tensor("s1_wq", [128, NC * E], BF16))
        wks = s2.enter_context(nc.sbuf_tensor("s1_wk", [128, NC * E], BF16))
        wvs = s2.enter_context(nc.sbuf_tensor("s1_wv", [128, NC * E], BF16))
        xt = [s2.enter_context(nc.sbuf_tensor(f"s1_xt{i}", [128, E], F32)) for i in range(2)]
        xcb = [s2.enter_context(nc.sbuf_tensor(f"s1_xc{i}", [128, E], F32)) for i in range(2)]
        xh = [s2.enter_context(nc.sbuf_tensor(f"s1_xh{i}", [128, E], BF16)) for i in range(2)]
        xT = [s2.enter_context(nc.sbuf_tensor(f"s1_xT{i}", [128, E], BF16)) for i in range(2)]
        ktmp = [s2.enter_context(nc.sbuf_tensor("s1_ktm0", [128, NC * 128], BF16))] * 2
        sml = s2.enter_context(nc.sbuf_tensor("s1_sm", [128, 6], F32))
        ps_tr = [s2.enter_context(nc.psum_tensor(f"s1_ptr{i}", [128, 128], BF16)) for i in range(2)]
        ps_k = [s2.enter_context(nc.psum_tensor(f"s1_pk{i}", [128, 512], F32)) for i in range(2)]
        ps_q = [s2.enter_context(nc.psum_tensor(f"s1_pq{i}", [128, 512], F32)) for i in range(2)]
        ps_v = [s2.enter_context(nc.psum_tensor(f"s1_pv{i}", [128, 512], F32)) for i in range(2)]

        bXp = (p.cnt["sX"], p.cnt["sX2"]); bV = p.cnt["sV"]; bS = p.cnt["sS"]
        bTt = p.cnt["sTt"]; bVt = p.cnt["sVt"]; bTm = p.cnt["sTm"]
        bE = p.cnt["sV2"]; bKQ = p.cnt["sKQ"]
        XSEM = lambda b: ("sX", "sX2")[b % 2]
        X_ = lambda b: bXp[b % 2] + 16 * (b // 2 + 1)
        V_ = lambda b, i: bV + 2 * b + i
        S_ = lambda b, i: bS + 2 * b + i
        Tt = lambda g: bTt + g + 1
        Vt = lambda g: bVt + g + 1
        Tm = lambda b, i: bTm + 3 * b + i
        E_ = lambda b, i: bE + 3 * b + i
        KQ = lambda b: bKQ + 16 * (b + 1)
        p.bump("sX", 16 * (NBLK // 2)); p.bump("sX2", 16 * (NBLK // 2))
        p.bump("sV", 2 * NBLK); p.bump("sS", 2 * NBLK)
        p.bump("sTt", 8 * NBLK); p.bump("sVt", 8 * NBLK); p.bump("sTm", 3 * NBLK)
        p.bump("sV2", 3 * NBLK); p.bump("sKQ", 16 * NBLK)
        wv_mark = p.bump("sC", 16 * 24)

        @block.gpsimd
        def _(g):
            for w_src, w_sb in ((wq_in, wqs), (wk_in, wks), (wv_in, wvs)):
                for c in range(NC):
                    g.dma_start(out=w_sb[:, c * E:(c + 1) * E],
                                in_=ap(w_src, c * 128 * E, [[E, 128], [1, E]])
                                ).then_inc(p.sem["sC"], 16)
            for b in range(NBLK):
                if b >= 2:
                    p.wait(g, "sS", S_(b - 2, 1))
                g.dma_start(out=xt[b % 2][:],
                            in_=ap(src, b * 128 * E, [[E, 128], [1, E]])).then_inc(p.sem[XSEM(b)], 16)

        @block.sync
        def _(sync):
            for b in range(NBLK):
                p.wait(sync, "sV2", E_(b, 1))
                sync.dma_start(out=ap(k1_hbm, b * 128, [[S, 128], [128 * S, NC], [1, 128]]),
                               in_=ktmp[b % 2][:]).then_inc(p.sem["sKQ"], 16)

        def kqv_evacs(v, b):
            bb = b % 2
            p.wait(v, "sTm", Tm(b, 1))
            if b >= 1:
                p.wait(v, "sKQ", KQ(b - 1))
            last = None
            for bank in range(2):
                for hp4 in range(4):
                    hp = bank * 4 + hp4
                    last = v.tensor_copy(ktmp[bb][:, hp * 128:(hp + 1) * 128],
                                         ps_k[bank][:, hp4 * 128:(hp4 + 1) * 128])
            last.then_inc(p.sem["sV2"], 1)
            p.wait(v, "sTm", Tm(b, 2))
            for bank in range(2):
                for hp4 in range(4):
                    hp = bank * 4 + hp4
                    src_ap = ap(ps_q[bank], hp4 * 128, [[512, 128], [16, 8], [1, 16]])
                    dst_ap = ap(q_sb, hp * S + 8 * b, [[NC * S, 128], [1, 8], [256, 16]])
                    last = v.tensor_copy(dst_ap, src_ap)
            last.then_inc(p.sem["sV2"], 1)
            p.wait(v, "sTm", Tm(b, 3))
            for half in range(2):
                src_ap = ap(ps_v[half], 0, [[512, 128], [64, 8], [1, 64]])
                dst_ap = ap(vaug, b * (H * 65) + half * 8 * 65,
                            [[NBLK * H * 65, 128], [65, 8], [1, 64]])
                last = v.tensor_copy(dst_ap, src_ap)
            last.then_inc(p.sem["sV2"], 1)

        @block.vector
        def _(v):
            v.memset(vaug[:], 1.0)
            for b in range(NBLK):
                bb = b % 2
                p.wait(v, XSEM(b), X_(b))
                if b >= 2:
                    p.wait(v, "sS", S_(b - 2, 1))
                ln_vec_pre(p, v, xt[bb][:], sml[:, bb:bb + 1], xcb[bb][:], "sV")
                p.wait(v, "sS", S_(b, 2))
                if b >= 2:
                    p.wait(v, "sTt", Tt((b - 2) * 8 + 7))
                v.tensor_scalar(xh[bb][:], xcb[bb][:], sml[:, 4 + bb:5 + bb], None,
                                ALU.mult).then_inc(p.sem["sV"], 1)
                for c in range(NC):
                    g = b * 8 + c
                    p.wait(v, "sTt", Tt(g))
                    v.tensor_copy(xT[bb][:, c * 128:(c + 1) * 128],
                                  ps_tr[c % 2][:]).then_inc(p.sem["sVt"], 1)
                if b >= 1:
                    kqv_evacs(v, b - 1)
            kqv_evacs(v, NBLK - 1)

        @block.scalar
        def _(sc):
            for b in range(NBLK):
                bb = b % 2
                p.wait(sc, "sV", V_(b, 1))
                if b >= 2:
                    p.wait(sc, "sV", V_(b - 2, 2))
                ln_scalar(p, sc, xcb[bb][:], xt[bb][:], sml[:, 2 + bb:3 + bb],
                          sml[:, 4 + bb:5 + bb], epsc[:, 0:1], "sS", S_(b, 1))

        @block.tensor
        def _(t):
            t.wait_ge(p.sem["sC"], wv_mark)
            t.wait_ge(p.sem["sW"], p.aux_constw)
            for b in range(NBLK):
                bb = b % 2
                p.wait(t, "sV", V_(b, 2))
                for c in range(NC):
                    g = b * 8 + c
                    if g >= 2:
                        p.wait(t, "sVt", Vt(g - 2))
                    t.transpose(ps_tr[c % 2][:], xh[bb][:, c * 128:(c + 1) * 128],
                                idb[:]).then_inc(p.sem["sTt"], 1)
                p.wait(t, "sVt", Vt(b * 8 + 7))
                if b >= 1:
                    p.wait(t, "sV2", E_(b - 1, 3))
                for wsb, psb in ((wks, ps_k), (wqs, ps_q)):
                    last = None
                    for bank in range(2):
                        for hp4 in range(4):
                            hp = bank * 4 + hp4
                            for c in range(NC):
                                last = t.matmul(psb[bank][:, hp4 * 128:(hp4 + 1) * 128],
                                                wsb[:, c * E + hp * 128: c * E + (hp + 1) * 128],
                                                xT[bb][:, c * 128:(c + 1) * 128],
                                                start=(hp4 == 0 and c == 0),
                                                stop=(hp4 == 3 and c == NC - 1))
                    last.then_inc(p.sem["sTm"], 1)
                last = None
                for half in range(2):
                    for c in range(NC):
                        last = t.matmul(ps_v[half][:],
                                        xT[bb][:, c * 128:(c + 1) * 128],
                                        wvs[:, c * E + half * 512: c * E + (half + 1) * 512],
                                        start=(c == 0), stop=(c == NC - 1))
                last.then_inc(p.sem["sTm"], 1)


    dstore = st.enter_context(nc.sbuf_tensor("s1_dstb", [128, 72 * 64], F32))
    cbc = st.enter_context(nc.sbuf_tensor("s1_cbc", [128, H * 64], F32))
    with contextlib.ExitStack() as s2, nc.Block() as block:
        ttile = [s2.enter_context(nc.sbuf_tensor(f"s1_tt{i}", [128, S], BF16)) for i in range(2)]
        ttc = s2.enter_context(nc.sbuf_tensor("s1_ttc", [128, 32 * 128], BF16))
        dtmp = s2.enter_context(nc.sbuf_tensor("s1_dtmp", [64, 128], BF16))
        crow = s2.enter_context(nc.sbuf_tensor("s1_crow", [128, 1], BF16))
        cvr = s2.enter_context(nc.sbuf_tensor("s1_cvr", [1, 64], BF16))
        ps_t1 = [s2.enter_context(nc.psum_tensor(f"s1_pt1{i}", [128, 128], BF16)) for i in range(2)]
        ps_d = s2.enter_context(nc.psum_tensor("s1_pd", [64, 128], F32))
        ps_d2 = s2.enter_context(nc.psum_tensor("s1_pd2", [128, 64], BF16))
        ps_c1 = s2.enter_context(nc.psum_tensor("s1_pc1", [1, 64], F32))
        ps_c2 = s2.enter_context(nc.psum_tensor("s1_pc2", [128, 64], F32))

        bWp = (p.cnt["sW"], p.cnt["sW2"]); bTt = p.cnt["sTt"]; bVt = p.cnt["sVt"]
        bTm = p.cnt["sTm"]; bE = p.cnt["sV2"]
        npair = [16 - g for g in range(8, 16)]
        WSEM = lambda it: ("sW", "sW2")[it % 2]
        W_ = lambda it: bWp[it % 2] + 16 * (it // 2 + 1)
        Tt = lambda q: bTt + q + 1
        Vt = lambda q: bVt + q + 1
        p.bump("sW", 16 * 8); p.bump("sW2", 16 * 8); p.bump("sTt", 32 * 16); p.bump("sVt", 32 * 16)
        crow_mark = p.bump("sKQ", 16)
        tm_marks = {}; v2_marks = {}
        tmc = bTm; v2c = bE
        for gi in range(8):
            for half in range(2):
                for pi in range(npair[gi]):
                    tm_marks[(gi, half, pi)] = (tmc + 1, tmc + 2); tmc += 2
                    v2_marks[(gi, half, pi)] = (v2c + 1, v2c + 2); v2c += 2
        cv_tm = {}; cv_v2 = {}
        for hh in range(H):
            cv_tm[hh] = (tmc + 1, tmc + 2); tmc += 2
            cv_v2[hh] = (v2c + 1, v2c + 2); v2c += 2
        p.bump("sTm", tmc - bTm); p.bump("sV2", v2c - bE)

        @block.sync
        def _(sync):
            sync.wait_ge(p.sem["sTG"], cast1)
            sync.dma_start(out=crow[:], in_=ap(tg_bf, 0, [[1, 128], [1, 1]])
                           ).then_inc(p.sem["sKQ"], 16)
            for gi in range(8):
                for half in range(2):
                    it = gi * 2 + half
                    if it >= 2:
                        p.wait(sync, "sTt", Tt((it - 1) * 32 - 1))
                    sync.dma_start(out=ttile[it % 2][:],
                                   in_=ap(tg_bf, (gi * 256 + half * 128) * S, [[S, 128], [1, S]])
                                   ).then_inc(p.sem[WSEM(it)], 16)

        @block.tensor
        def _(t):
            for gi in range(8):
                for half in range(2):
                    it = gi * 2 + half
                    p.wait(t, WSEM(it), W_(it))
                    for c in range(32):
                        q = it * 32 + c
                        if q >= 2:
                            p.wait(t, "sVt", Vt(q - 2))
                        t.transpose(ps_t1[c % 2][:], ttile[it % 2][:, c * 128:(c + 1) * 128],
                                    idb[:]).then_inc(p.sem["sTt"], 1)
                    p.wait(t, "sVt", Vt(it * 32 + 31))
                    for pi in range(npair[gi]):
                        hh = pi + 8 + gi
                        last = None
                        for c in range(32):
                            last = t.matmul(ps_d[:],
                                            vaug[:, c * (H * 65) + hh * 65: c * (H * 65) + hh * 65 + 64],
                                            ttc[:, c * 128:(c + 1) * 128],
                                            start=(c == 0), stop=(c == 31))
                        last.then_inc(p.sem["sTm"], 1)
                        p.wait(t, "sV2", v2_marks[(gi, half, pi)][0])
                        t.transpose(ps_d2[:], dtmp[:], idb[0:64, 0:64]).then_inc(p.sem["sTm"], 1)
            p.wait(t, "sKQ", crow_mark)
            for hh in range(H):
                last = None
                for c in range(32):
                    last = t.matmul(ps_c1[:], crow[:],
                                    vaug[:, c * (H * 65) + hh * 65: c * (H * 65) + hh * 65 + 64],
                                    start=(c == 0), stop=(c == 31))
                last.then_inc(p.sem["sTm"], 1)
                p.wait(t, "sV2", cv_v2[hh][0])
                t.matmul(ps_c2[:], onesb[:], cvr[:], start=True, stop=True).then_inc(p.sem["sTm"], 1)

        @block.vector
        def _(v):
            for gi in range(8):
                for half in range(2):
                    it = gi * 2 + half
                    for c in range(32):
                        q = it * 32 + c
                        p.wait(v, "sTt", Tt(q))
                        v.tensor_copy(ttc[:, c * 128:(c + 1) * 128],
                                      ps_t1[c % 2][:]).then_inc(p.sem["sVt"], 1)
                    for pi in range(npair[gi]):
                        p.wait(v, "sTm", tm_marks[(gi, half, pi)][0])
                        v.tensor_copy(dtmp[:], ps_d[:]).then_inc(p.sem["sV2"], 1)
                        p.wait(v, "sTm", tm_marks[(gi, half, pi)][1])
                        pid = PAIR_IDX[(pi + 8 + gi, pi)]
                        v.tensor_copy(dstore[:, (pid * 2 + half) * 64:(pid * 2 + half + 1) * 64],
                                      ps_d2[:]).then_inc(p.sem["sV2"], 1)
            for hh in range(H):
                p.wait(v, "sTm", cv_tm[hh][0])
                v.tensor_copy(cvr[:], ps_c1[:]).then_inc(p.sem["sV2"], 1)
                p.wait(v, "sTm", cv_tm[hh][1])
                v.tensor_copy(cbc[:, hh * 64:(hh + 1) * 64], ps_c2[:]).then_inc(p.sem["sV2"], 1)
        p.check("sV2", v2c)

    with contextlib.ExitStack() as s2, nc.Block() as block:
        kst = [s2.enter_context(nc.sbuf_tensor(f"s1_kst{i}", [128, S], BF16)) for i in range(2)]
        pt = [s2.enter_context(nc.sbuf_tensor(f"s1_pts{i}", [128, 1024], BF16)) for i in range(2)]
        outT = [s2.enter_context(nc.sbuf_tensor(f"s1_oT{i}", [128, 512], F32)) for i in range(2)]
        att = [s2.enter_context(nc.sbuf_tensor(f"s1_att{i}", [128, E], F32)) for i in range(4)]
        xr = s2.enter_context(nc.sbuf_tensor("s1_xr", [128, E], F32))
        gam = s2.enter_context(nc.sbuf_tensor("s1_gam", [128, E], BF16))
        bet = s2.enter_context(nc.sbuf_tensor("s1_bet", [128, E], BF16))
        sdump = s2.enter_context(nc.sbuf_tensor("s1_sdmp", [128, E], F32))
        sml = s2.enter_context(nc.sbuf_tensor("s1_sm2", [128, 12], F32))
        rc = s2.enter_context(nc.sbuf_tensor("s1_rc", [128, 2], F32))
        ps_s = [s2.enter_context(nc.psum_tensor(f"s1_pss{i}", [128, 1024], F32)) for i in range(2)]
        ps_o = [s2.enter_context(nc.psum_tensor(f"s1_po{i}", [65, 512], F32)) for i in range(2)]
        ps_a = [s2.enter_context(nc.psum_tensor(f"s1_pa{i}", [128, 128], F32)) for i in range(2)]
        zb = s2.enter_context(nc.sbuf_tensor("s1_zb", [128, 128], BF16))

        bKQp = (p.cnt["sKQ"], p.cnt["sKQ2"]); bCRp = (p.cnt["sCR"], p.cnt["sCR2"])
        bST = p.cnt["sST"]; bP = p.cnt["sP"]
        bPV = p.cnt["sPV"]; bOE = p.cnt["sOE"]; bTt = p.cnt["sTt"]; bE = p.cnt["sV2"]
        bX = p.cnt["sX"]; bV = p.cnt["sV"]
        gb_mark = p.bump("sGD", 32)
        bD = p.cnt["sGD"]
        bLx = p.cnt["sLx"]; bLr = p.cnt["sLr"]
        MP = 64
        KQAS = lambda mp: ("sKQ", "sKQ2")[mp % 2]
        KQAV = lambda mp: bKQp[mp % 2] + 16 * (mp // 2 + 1)
        KQBS = lambda mp: ("sCR", "sCR2")[mp % 2]
        KQBV = lambda mp: bCRp[mp % 2] + 16 * (mp // 2 + 1)
        STm = lambda mp, j, h: bST + 32 * mp + 2 * j + 1 + h
        EXm = lambda mp, j, h: bP + 32 * mp + 2 * j + 1 + h
        PVm = lambda mp, j, h: bPV + 32 * mp + 2 * j + 1 + h
        OEm = lambda mp, h: bOE + 2 * mp + 1 + h
        Tt = lambda u: bTt + u + 1
        E_ = lambda u: bE + u + 1
        XR = lambda i: bX + 16 * (i + 1)
        D_ = lambda i: bD + 16 * (i + 1)
        XN = lambda i: bV + i + 1
        Lx = lambda i: bLx + i + 1
        Lr = lambda i: bLr + 2 * (i + 1)
        p.bump("sKQ", 16 * (MP // 2)); p.bump("sKQ2", 16 * (MP // 2))
        p.bump("sCR", 16 * (MP // 2)); p.bump("sCR2", 16 * (MP // 2))
        p.bump("sST", 32 * MP); p.bump("sP", 32 * MP); p.bump("sPV", 32 * MP)
        p.bump("sOE", 2 * MP); p.bump("sTt", 8 * MP); p.bump("sV2", 8 * MP)
        p.bump("sX", 16 * 32); p.bump("sGD", 16 * 32); p.bump("sV", 32)
        p.bump("sLx", 32); p.bump("sLr", 64)

        @block.sync
        def _(sync):
            for mp in range(MP):
                qg, hhp = divmod(mp, 8)
                hp = hhp
                if mp >= 2:
                    p.wait(sync, "sST", STm(mp - 2, 15, 1))
                sync.dma_start(out=kst[mp % 2][0:64, :],
                               in_=ap(k1_hbm, hp * 128 * S, [[S, 64], [1, S]])
                               ).then_inc(p.sem[KQAS(mp)], 16)
                sync.dma_start(out=kst[mp % 2][64:128, :],
                               in_=ap(k1_hbm, hp * 128 * S + 64 * S, [[S, 64], [1, S]])
                               ).then_inc(p.sem[KQBS(mp)], 16)

        @block.gpsimd
        def _(g):
            g.dma_start(out=gam[:], in_=gamma_bc_in).then_inc(p.sem["sGD"], 16)
            g.dma_start(out=bet[:], in_=beta_bc_in).then_inc(p.sem["sGD"], 16)

            def store(i):
                qg, c = divmod(i, 4)
                qm = 2 * qg + c // 2; qdh = c % 2
                p.wait(g, "sV", XN(i))
                g.dma_start(out=ap(dst, (2048 * qdh + qm) * E, [[16 * E, 128], [1, E]]),
                            in_=xr[:]).then_inc(p.sem["sGD"], 16)
            for i in range(32):
                qg, c = divmod(i, 4)
                qm = 2 * qg + c // 2; qdh = c % 2
                if i >= 1:
                    store(i - 1)
                    p.wait(g, "sGD", D_(i - 1))
                g.dma_start(out=xr[:],
                            in_=ap(src, (2048 * qdh + qm) * E, [[16 * E, 128], [1, E]])
                            ).then_inc(p.sem["sX"], 16)
            store(31)

        @block.tensor
        def _(t):
            t.wait_ge(p.sem["sZB"], p.bump("sZB", 1))
            fk = [0]

            def fill(n):
                for _ in range(n):
                    o = (fk[0] % 7) * 512
                    t.matmul(ps_o[fk[0] % 2][:], zb[:, 0:65], q_sb[:, o:o + 512],
                             start=False, stop=False, skip_group_check=True)
                    fk[0] += 1

            for mp in range(MP):
                qg, hhp = divmod(mp, 8)
                hp = hhp

                def s_grp(j, h):
                    hq = h * 64
                    if mp > 0 or j >= 1:
                        pj, pm = (j - 1, mp) if j >= 1 else (15, mp - 1)
                        p.wait(t, "sP", EXm(pm, pj, h))
                    for h2 in range(2):
                        kt = 2 * j + h2
                        ins = t.matmul(ps_s[h][:, h2 * 512:(h2 + 1) * 512],
                                       kst[mp % 2][hq:hq + 64, kt * 128:(kt + 1) * 128],
                                       q_sb[hq:hq + 64, hp * S + qg * 512: hp * S + (qg + 1) * 512],
                                       start=True, stop=True)
                        if h2 == 1:
                            ins.then_inc(p.sem["sST"], 1)

                def pv_grp(j, h):
                    hh = 2 * hhp + h
                    p.wait(t, "sP", EXm(mp, j, h))
                    if j == 0 and mp >= 1:
                        p.wait(t, "sOE", OEm(mp - 1, h))
                    for h2 in range(2):
                        kt = 2 * j + h2
                        ins = t.matmul(ps_o[h][:],
                                       vaug[:, kt * (H * 65) + hh * 65: kt * (H * 65) + (hh + 1) * 65],
                                       pt[h][:, h2 * 512:(h2 + 1) * 512],
                                       start=(kt == 0), stop=(kt == 31))
                        if h2 == 1:
                            ins.then_inc(p.sem["sPV"], 1)

                p.wait(t, KQAS(mp), KQAV(mp))
                p.wait(t, KQBS(mp), KQBV(mp))
                s_grp(0, 0)
                s_grp(0, 1)
                if mp >= 1:
                    for c in range(8):
                        u = 8 * (mp - 1) + c
                        p.wait(t, "sOE", OEm(mp - 1, c // 4))
                        if u >= 2:
                            p.wait(t, "sV2", E_(u - 2))
                        t.transpose(ps_a[u % 2][:],
                                    outT[c // 4][:, (c % 4) * 128:(c % 4 + 1) * 128],
                                    idf[:]).then_inc(p.sem["sTt"], 1)
                for j in range(1, 16):
                    if j >= 2:
                        fill(3)
                    s_grp(j, 0)
                    s_grp(j, 1)
                    pv_grp(j - 1, 0)
                    pv_grp(j - 1, 1)
                pv_grp(15, 0)
                pv_grp(15, 1)
            for c in range(8):
                u = 8 * (MP - 1) + c
                p.wait(t, "sOE", OEm(MP - 1, c // 4))
                p.wait(t, "sV2", E_(u - 2))
                t.transpose(ps_a[u % 2][:],
                            outT[c // 4][:, (c % 4) * 128:(c % 4 + 1) * 128],
                            idf[:]).then_inc(p.sem["sTt"], 1)

        @block.scalar
        def _(sc):
            for mp in range(MP):
                qg, hhp = divmod(mp, 8)
                for j in range(16):
                    for h in range(2):
                        p.wait(sc, "sST", STm(mp, j, h))
                        if mp > 0 or j >= 1:
                            pj, pm = (j - 1, mp) if j >= 1 else (15, mp - 1)
                            p.wait(sc, "sPV", PVm(pm, pj, h))
                        sc.activation(pt[h][:], ps_s[h][:], AF.Exp,
                                      scale=1.0 / 32.0).then_inc(p.sem["sP"], 1)
                if hhp == 7:
                    for c in range(4):
                        i = 4 * qg + c
                        p.wait(sc, "sLx", Lx(i))
                        ln_scalar(p, sc, att[c][:], sdump[:], sml[:, 4 + c:5 + c],
                                  sml[:, 8 + c:9 + c], epsc[:, 1:2], "sLr", Lr(i) - 1)

        @block.vector
        def _(v):
            v.memset(zb[:], 0.0).then_inc(p.sem["sZB"], 1)
            cc = p.inc(v.memset(outT[0][:], 0.0), "sVc")
            p.wait(v, "sVc", cc)
            cc = p.inc(v.memset(outT[1][:], 0.0), "sVc")
            p.wait(v, "sVc", cc)
            for mp in range(MP):
                qg, hhp = divmod(mp, 8)
                for h in range(2):
                    p.wait(v, "sPV", PVm(mp, 15, h))
                    if mp >= 1:
                        p.wait(v, "sTt", Tt(8 * (mp - 1) + 4 * h + 3))
                    v.tensor_copy(outT[h][0:65, :], ps_o[h][:]).then_inc(p.sem["sOE"], 1)
                for c in range(8):
                    u = 8 * mp + c
                    h = c // 4
                    hh = 2 * hhp + h
                    cc4 = c % 4
                    qm = 2 * qg + cc4 // 2; qdh = cc4 % 2
                    p.wait(v, "sTt", Tt(u))
                    if mp >= 8 and hhp == 0:
                        p.wait(v, "sLr", Lr(4 * (qg - 1) + cc4))
                    cc = p.inc(v.reciprocal(rc[:, u % 2:u % 2 + 1], ps_a[u % 2][:, 64:65]), "sVc")
                    p.wait(v, "sVc", cc)
                    if (hh, qm) in PAIR_IDX:
                        pid = PAIR_IDX[(hh, qm)]
                        bias_ap = dstore[:, (pid * 2 + qdh) * 64:(pid * 2 + qdh + 1) * 64]
                    else:
                        bias_ap = cbc[:, hh * 64:(hh + 1) * 64]
                    v.scalar_tensor_tensor(att[cc4][:, hh * 64:(hh + 1) * 64],
                                           ps_a[u % 2][:, 0:64], rc[:, u % 2:u % 2 + 1],
                                           bias_ap, ALU.mult, ALU.add).then_inc(p.sem["sV2"], 1)
                if hhp == 7:
                    if qg == 0:
                        p.wait(v, "sGD", gb_mark)
                    for c in range(4):
                        i = 4 * qg + c
                        ln_vec_pre(p, v, att[c][:], sml[:, c:c + 1], att[c][:], "sLx")
                    for c in range(4):
                        i = 4 * qg + c
                        p.wait(v, "sLr", Lr(i))
                        cc = p.inc(v.tensor_scalar(att[c][:], att[c][:], sml[:, 8 + c:9 + c], None,
                                                   ALU.mult), "sVc")
                        p.wait(v, "sVc", cc)
                        cc = p.inc(v.tensor_mul(att[c][:], att[c][:], gam[:]), "sVc")
                        p.wait(v, "sVc", cc)
                        cc = p.inc(v.tensor_add(att[c][:], att[c][:], bet[:]), "sVc")
                        p.wait(v, "sVc", cc)
                        p.wait(v, "sX", XR(i))
                        v.tensor_add(xr[:], att[c][:], xr[:]).then_inc(p.sem["sV"], 1)


def emit_moe(nc, p, src, dst, wgu_bf, wd_bf, wr_in, ct_dram, idb, idf, onesb, epsc, cast_all, tag, gate):
    with contextlib.ExitStack() as st:
        xTf = st.enter_context(nc.sbuf_tensor(f"{tag}_xT", [128, NC * S], BF16))

        with contextlib.ExitStack() as s2, nc.Block() as block:
            xt = [s2.enter_context(nc.sbuf_tensor(f"{tag}_xt{i}", [128, E], F32)) for i in range(2)]
            xcb = [s2.enter_context(nc.sbuf_tensor(f"{tag}_xc{i}", [128, E], F32)) for i in range(2)]
            xh = [s2.enter_context(nc.sbuf_tensor(f"{tag}_xh{i}", [128, E], BF16)) for i in range(2)]
            sqd = s2.enter_context(nc.sbuf_tensor(f"{tag}_sqd", [128, E], F32))
            sml = s2.enter_context(nc.sbuf_tensor(f"{tag}_sm", [128, 6], F32))
            wrf = s2.enter_context(nc.sbuf_tensor(f"{tag}_wrf", [128, NC * 8], F32))
            wrs = s2.enter_context(nc.sbuf_tensor(f"{tag}_wrs", [128, NC * 8], BF16))
            lg = s2.enter_context(nc.sbuf_tensor(f"{tag}_lg", [128, NBLK * 8], F32))
            lg2 = s2.enter_context(nc.sbuf_tensor(f"{tag}_lg2", [128, NBLK * 8], F32))
            eqt = s2.enter_context(nc.sbuf_tensor(f"{tag}_eqt", [128, NBLK * 8], F32))
            tps = s2.enter_context(nc.sbuf_tensor(f"{tag}_tps", [128, NBLK * 4], F32))
            ps_tr = [s2.enter_context(nc.psum_tensor(f"{tag}_ptr{i}", [128, 128], BF16)) for i in range(2)]
            ps_lg = s2.enter_context(nc.psum_tensor(f"{tag}_plg", [128, 8], F32))
            ps_ct = [s2.enter_context(nc.psum_tensor(f"{tag}_pct{i}", [8, 128], F32)) for i in range(2)]
            cttmp = [s2.enter_context(nc.sbuf_tensor(f"{tag}_ctt{i}", [8, 128], BF16)) for i in range(2)]

            bXp = (p.cnt["sX"], p.cnt["sX2"]); bV = p.cnt["sV"]; bS = p.cnt["sS"]
            bTt = p.cnt["sTt"]; bVt = p.cnt["sVt"]; bTm = p.cnt["sTm"]; bE = p.cnt["sV2"]
            bOE = p.cnt["sOE"]; bLr = p.cnt["sLr"]
            XSEM = lambda b: ("sX", "sX2")[b % 2]
            X_ = lambda b: bXp[b % 2] + 16 * (b // 2 + 1)
            V_ = lambda b, i: bV + 2 * b + i
            S_ = lambda b, i: bS + 2 * b + i
            Tt = lambda g: bTt + g + 1
            Vt = lambda g: bVt + g + 1
            Tm = lambda b: bTm + b + 1
            E_ = lambda b: bE + b + 1
            p.bump("sX", 16 * (NBLK // 2)); p.bump("sX2", 16 * (NBLK // 2))
            p.bump("sV", 2 * NBLK); p.bump("sS", 2 * NBLK)
            p.bump("sTt", 8 * NBLK); p.bump("sVt", 8 * NBLK)
            p.bump("sTm", NBLK); p.bump("sV2", NBLK)
            wr_mark = p.bump("sW", 16)
            ST5 = bOE + 1; ST6 = bOE + 2; ST9 = bOE + 3
            p.bump("sOE", 3)
            EXD = bLr + 1
            p.bump("sLr", 1)
            bTm2 = p.cnt["sTm"]; bE2 = p.cnt["sV2"]
            TmC = lambda b: bTm2 + b + 1
            E_C = lambda b: bE2 + b + 1
            p.bump("sTm", NBLK); p.bump("sV2", NBLK)
            bDDp = (p.cnt["sD"], p.cnt["sD2"])
            CTSEM = lambda b: ("sD", "sD2")[b % 2]
            CTS = lambda b: bDDp[b % 2] + 16 * (b // 2 + 1)
            p.bump("sD", 16 * (NBLK // 2)); p.bump("sD2", 16 * (NBLK // 2))

            @block.sync
            def _(sync):
                sync.wait_ge(p.sem["sTG"], cast_all[0])
                sync.wait_ge(p.sem["sC2"], cast_all[1])
                sync.dma_start(out=wrf[:], in_=ap(wr_in, 0, [[8, 128], [128 * 8, NC], [1, 8]])
                               ).then_inc(p.sem["sW"], 16)
                for b in range(NBLK):
                    p.wait(sync, "sV2", E_C(b))
                    sync.dma_start(out=ap(ct_dram, b * 128, [[S, 8], [1, 128]]),
                                   in_=cttmp[b % 2][:]).then_inc(p.sem[CTSEM(b)], 16)

            @block.gpsimd
            def _(g):
                p.wait(g, "sD", gate[0])
                p.wait(g, "sD2", gate[1])
                p.wait(g, "sGD", gate[2])
                for b in range(NBLK):
                    if b >= 2:
                        p.wait(g, "sS", S_(b - 2, 1))
                    g.dma_start(out=xt[b % 2][:],
                                in_=ap(src, b * 128 * E, [[E, 128], [1, E]])).then_inc(p.sem[XSEM(b)], 16)

            @block.vector
            def _(v):
                p.wait(v, "sW", wr_mark)
                v.tensor_copy(wrs[:], wrf[:])
                for b in range(NBLK):
                    bb = b % 2
                    p.wait(v, XSEM(b), X_(b))
                    if b >= 2:
                        p.wait(v, "sS", S_(b - 2, 1))
                    ln_vec_pre(p, v, xt[bb][:], sml[:, bb:bb + 1], xcb[bb][:], "sV")
                    p.wait(v, "sS", S_(b, 2))
                    if b >= 2:
                        p.wait(v, "sTt", Tt((b - 2) * 8 + 7))
                    v.tensor_scalar(xh[bb][:], xcb[bb][:], sml[:, 4 + bb:5 + bb], None,
                                    ALU.mult).then_inc(p.sem["sV"], 1)
                    for c in range(NC):
                        g = b * 8 + c
                        p.wait(v, "sTt", Tt(g))
                        v.tensor_copy(xTf[:, c * S + b * 128:c * S + (b + 1) * 128],
                                      ps_tr[c % 2][:]).then_inc(p.sem["sVt"], 1)
                    if b >= 1:
                        p.wait(v, "sTm", Tm(b - 1))
                        v.tensor_copy(lg[:, 8 * (b - 1):8 * b], ps_lg[:]).then_inc(p.sem["sV2"], 1)
                p.wait(v, "sTm", Tm(NBLK - 1))
                v.tensor_copy(lg[:, 8 * (NBLK - 1):8 * NBLK], ps_lg[:]).then_inc(p.sem["sV2"], 1)

            @block.scalar
            def _(sc):
                for b in range(NBLK):
                    bb = b % 2
                    p.wait(sc, "sV", V_(b, 1))
                    if b >= 2:
                        p.wait(sc, "sV", V_(b - 2, 2))
                    ln_scalar(p, sc, xcb[bb][:], sqd[:], sml[:, 2 + bb:3 + bb],
                              sml[:, 4 + bb:5 + bb], epsc[:, 0:1], "sS", S_(b, 1))
                p.wait(sc, "sOE", ST5)
                sc.activation(tps[:, 2 * NBLK:3 * NBLK], tps[:, 2 * NBLK:3 * NBLK],
                              AF.Exp).then_inc(p.sem["sLr"], 1)

            @block.tensor
            def _(t):
                for b in range(NBLK):
                    bb = b % 2
                    p.wait(t, "sV", V_(b, 2))
                    for c in range(NC):
                        g = b * 8 + c
                        if g >= 2:
                            p.wait(t, "sVt", Vt(g - 2))
                        t.transpose(ps_tr[c % 2][:], xh[bb][:, c * 128:(c + 1) * 128],
                                    idb[:]).then_inc(p.sem["sTt"], 1)
                    p.wait(t, "sVt", Vt(b * 8 + 7))
                    if b >= 1:
                        p.wait(t, "sV2", E_(b - 1))
                    last = None
                    for c in range(NC):
                        last = t.matmul(ps_lg[:], xTf[:, c * S + b * 128:c * S + (b + 1) * 128],
                                        wrs[:, c * 8:(c + 1) * 8],
                                        start=(c == 0), stop=(c == NC - 1))
                    last.then_inc(p.sem["sTm"], 1)
                p.wait(t, "sOE", ST9)
                for b in range(NBLK):
                    if b >= 2:
                        p.wait(t, "sV2", E_C(b - 2))
                    t.transpose(ps_ct[b % 2][:], eqt[:, 8 * b:8 * (b + 1)], idf[:]
                                ).then_inc(p.sem["sTm"], 1)

            @block.vector
            def _(v):
                N2 = NBLK
                cc = p.inc(v.memset(tps[:, 0:1], 0.0), "sVc")
                p.wait(v, "sV2", E_(N2 - 1))
                p.wait(v, "sVc", cc)

                def stage(opfn):
                    for b in range(N2):
                        ins = opfn(b)
                    c2 = p.inc(ins, "sVc")
                    p.wait(v, "sVc", c2)

                stage(lambda b: v.reduce_max(tps[:, b:b + 1], lg[:, 8 * b:8 * (b + 1)], axis=AX))
                stage(lambda b: v.tensor_scalar(eqt[:, 8 * b:8 * (b + 1)], lg[:, 8 * b:8 * (b + 1)],
                                                tps[:, b:b + 1], None, ALU.is_equal))
                stage(lambda b: v.scalar_tensor_tensor(lg2[:, 8 * b:8 * (b + 1)], eqt[:, 8 * b:8 * (b + 1)],
                                                       NEG, lg[:, 8 * b:8 * (b + 1)], ALU.mult, ALU.add))
                stage(lambda b: v.reduce_max(tps[:, N2 + b:N2 + b + 1], lg2[:, 8 * b:8 * (b + 1)], axis=AX))
                for b in range(N2):
                    ins = v.tensor_sub(tps[:, 2 * N2 + b:2 * N2 + b + 1],
                                       tps[:, N2 + b:N2 + b + 1], tps[:, b:b + 1])
                ins.then_inc(p.sem["sOE"], 1)
                p.wait(v, "sLr", EXD)
                c = p.inc(v.tensor_scalar(tps[:, 3 * N2:4 * N2], tps[:, 2 * N2:3 * N2], 1.0, None,
                                          ALU.add), "sVc")
                p.wait(v, "sVc", c)
                c = p.inc(v.reciprocal(tps[:, 3 * N2:4 * N2], tps[:, 3 * N2:4 * N2]), "sVc")
                p.wait(v, "sVc", c)
                c = p.inc(v.tensor_scalar_mul(tps[:, 3 * N2:4 * N2], tps[:, 3 * N2:4 * N2], DS), "sVc")
                p.wait(v, "sVc", c)
                c = p.inc(v.tensor_mul(tps[:, 2 * N2:3 * N2], tps[:, 2 * N2:3 * N2],
                                       tps[:, 3 * N2:4 * N2]), "sVc")
                p.wait(v, "sVc", c)
                v.memset(tps[:, 0:1], 0.0).then_inc(p.sem["sOE"], 1)
                stage(lambda b: v.tensor_scalar(eqt[:, 8 * b:8 * (b + 1)], eqt[:, 8 * b:8 * (b + 1)],
                                                tps[:, 3 * N2 + b:3 * N2 + b + 1], None, ALU.mult))
                stage(lambda b: v.tensor_scalar(lg2[:, 8 * b:8 * (b + 1)], lg[:, 8 * b:8 * (b + 1)],
                                                tps[:, N2 + b:N2 + b + 1], None, ALU.is_equal))
                for b in range(N2):
                    ins = v.scalar_tensor_tensor(eqt[:, 8 * b:8 * (b + 1)], lg2[:, 8 * b:8 * (b + 1)],
                                                 tps[:, 2 * N2 + b:2 * N2 + b + 1],
                                                 eqt[:, 8 * b:8 * (b + 1)], ALU.mult, ALU.add)
                ins.then_inc(p.sem["sOE"], 1)
                for b in range(N2):
                    p.wait(v, "sTm", TmC(b))
                    if b >= 2:
                        p.wait(v, CTSEM(b - 2), CTS(b - 2))
                    v.tensor_copy(cttmp[b % 2][:], ps_ct[b % 2][:]).then_inc(p.sem["sV2"], 1)
            a_final_vt = p.cnt["sVt"]
            a_final_v2 = p.cnt["sV2"]
            a_cts = (p.cnt["sD"], p.cnt["sD2"])

        with contextlib.ExitStack() as s2, nc.Block() as block:
            h_sb = s2.enter_context(nc.sbuf_tensor(f"{tag}_h", [128, 32 * 512], BF16))
            oacc = s2.enter_context(nc.sbuf_tensor(f"{tag}_oacc", [128, NC * 512], F32))
            otmp = s2.enter_context(nc.sbuf_tensor(f"{tag}_otmp", [128, 4 * 512], F32))
            cbce = s2.enter_context(nc.sbuf_tensor(f"{tag}_cbc", [128, NE * 512], F32))
            wgs = [s2.enter_context(nc.sbuf_tensor(f"{tag}_wg{i}", [128, 4 * 1024], BF16)) for i in range(2)]
            wus = [s2.enter_context(nc.sbuf_tensor(f"{tag}_wu{i}", [128, 4 * 1024], BF16)) for i in range(2)]
            wds = [s2.enter_context(nc.sbuf_tensor(f"{tag}_wd{i}", [128, 4 * 512], BF16)) for i in range(2)]
            sg = [s2.enter_context(nc.sbuf_tensor(f"{tag}_sg{i}", [128, 512], BF16)) for i in range(2)]
            crw = [s2.enter_context(nc.sbuf_tensor(f"{tag}_crw{i}", [1, NE * 512], BF16)) for i in range(2)]
            xr = s2.enter_context(nc.sbuf_tensor(f"{tag}_xr", [128, E], F32))
            xnw = s2.enter_context(nc.sbuf_tensor(f"{tag}_xnw", [128, E], F32))
            ps_g = s2.enter_context(nc.psum_tensor(f"{tag}_pg", [128, 512], F32))
            ps_u = s2.enter_context(nc.psum_tensor(f"{tag}_pu", [128, 512], F32))
            ps_o = [s2.enter_context(nc.psum_tensor(f"{tag}_po{i}", [128, 512], F32)) for i in range(4)]
            ps_x = [s2.enter_context(nc.psum_tensor(f"{tag}_px{i}", [128, 512], F32)) for i in range(2)]

            bWp = [p.cnt["sW"], p.cnt["sW2"]]; bKQp = [p.cnt["sKQ"], p.cnt["sKQ2"]]
            bCRp = [p.cnt["sCR"], p.cnt["sCR2"]]
            bTm = p.cnt["sTm"]; bP = p.cnt["sP"]; bVt = p.cnt["sVt"]; bPV = p.cnt["sPV"]
            bE = p.cnt["sV2"]; bTt = p.cnt["sTt"]; bX = p.cnt["sX"]; bD = p.cnt["sGD"]
            mWG = {}; mWD = {}; mGU = {}; mSI = {}; mH = {}; mDG = {}; mOE = {}; mCB_t = {}; mCB_v = {}
            mTR = {}; mAD = {}; mXR = {}; mST = {}; mXN = {}
            cWp = list(bWp); cKQp = list(bKQp); cCRp = list(bCRp)
            cTm = bTm; cP = bP; cVt = bVt; cPV = bPV
            cE = bE; cTt = bTt; cX = bX; cD = bD
            jWG = 0; jWD = 0
            mCR = {}
            bOEm = p.cnt["sOE"]
            OAC = lambda tt: bOEm + tt + 1
            p.bump("sOE", 8)
            WGSEM = lambda j: ("sW", "sW2")[j % 2]
            WDSEM = lambda j: ("sKQ", "sKQ2")[j % 2]
            CRSEM = lambda tt: ("sCR", "sCR2")[tt % 2]
            for tt in range(8):
                cCRp[tt % 2] += 16
                mCR[tt] = cCRp[tt % 2]
                for e in range(NE):
                    mCB_t[(tt, e)] = cTm = cTm + 1
                    mCB_v[(tt, e)] = cE = cE + 1
                for e in range(NE):
                    for grp in range(8):
                        cWp[jWG % 2] += 32
                        mWG[(tt, e, grp)] = (jWG, cWp[jWG % 2])
                        jWG += 1
                    for ht in range(32):
                        mGU[(tt, e, ht, 0)] = cTm = cTm + 1
                        mGU[(tt, e, ht, 1)] = cTm = cTm + 1
                        mSI[(tt, e, ht)] = cP = cP + 1
                        mH[(tt, e, ht)] = cVt = cVt + 1
                    for half in range(2):
                        for g4 in range(8):
                            cKQp[jWD % 2] += 16
                            mWD[(tt, e, half, g4)] = (jWD, cKQp[jWD % 2])
                            jWD += 1
                            mDG[(tt, e, half, g4)] = cPV = cPV + 1
                        mOE[(tt, e, half)] = cE = cE + 1
                for r in range(4):
                    mXR[(tt, r)] = cX = cX + 16
                    for c in range(8):
                        mTR[(tt, r, c)] = cTt = cTt + 1
                        mAD[(tt, r, c)] = cE = cE + 1
                    mXN[(tt, r)] = mAD[(tt, r, 7)]
                    mST[(tt, r)] = cD = cD + 16
            p.bump("sW", cWp[0] - bWp[0]); p.bump("sW2", cWp[1] - bWp[1])
            p.bump("sKQ", cKQp[0] - bKQp[0]); p.bump("sKQ2", cKQp[1] - bKQp[1])
            p.bump("sCR", cCRp[0] - bCRp[0]); p.bump("sCR2", cCRp[1] - bCRp[1])
            p.bump("sTm", cTm - bTm)
            p.bump("sP", cP - bP); p.bump("sVt", cVt - bVt); p.bump("sPV", cPV - bPV)
            p.bump("sV2", cE - bE); p.bump("sTt", cTt - bTt); p.bump("sX", cX - bX)
            p.bump("sGD", cD - bD)
            WG_ORD = sorted(mWG.items(), key=lambda kv: kv[1][0])
            GU_LAST = {}
            for (tt, e, grp), (j, _) in mWG.items():
                GU_LAST[j] = mGU[(tt, e, grp * 4 + 3, 1)]
            DG_BYJ = {}
            for (tt, e, half, g4), (j, _) in mWD.items():
                DG_BYJ[j] = mDG[(tt, e, half, g4)]

            @block.sync
            def _(sync):
                sync.wait_ge(p.sem["sD"], a_cts[0])
                sync.wait_ge(p.sem["sD2"], a_cts[1])
                done_cr = set()
                for (tt, e, grp), (j, wv) in WG_ORD:
                    if tt not in done_cr:
                        done_cr.add(tt)
                        if tt >= 2:
                            p.wait(sync, "sTm", mCB_t[(tt - 2, NE - 1)])
                        sync.dma_start(out=crw[tt % 2][:],
                                       in_=ap(ct_dram, tt * 512, [[1, 1], [S, NE], [1, 512]])
                                       ).then_inc(p.sem[CRSEM(tt)], 16)
                    if j >= 2:
                        p.wait(sync, "sTm", GU_LAST[j - 2])
                    base_g = ((0 * NE + e) * 32 + grp * 4) * 128 * 1024
                    base_u = ((1 * NE + e) * 32 + grp * 4) * 128 * 1024
                    sync.dma_start(out=wgs[j % 2][:],
                                   in_=ap(wgu_bf, base_g, [[1024, 128], [128 * 1024, 4], [1, 1024]])
                                   ).then_inc(p.sem[WGSEM(j)], 16)
                    sync.dma_start(out=wus[j % 2][:],
                                   in_=ap(wgu_bf, base_u, [[1024, 128], [128 * 1024, 4], [1, 1024]])
                                   ).then_inc(p.sem[WGSEM(j)], 16)
                    if grp == 7:
                        for half in range(2):
                            for g4 in range(8):
                                j2, wv2 = mWD[(tt, e, half, g4)]
                                if j2 >= 2:
                                    p.wait(sync, "sPV", DG_BYJ[j2 - 2])
                                sync.dma_start(out=wds[j2 % 2][:],
                                               in_=ap(wd_bf, e * HID * E + g4 * 512 * E + half * 512,
                                                      [[E, 128], [128 * E, 4], [1, 512]])
                                               ).then_inc(p.sem[WDSEM(j2)], 16)

            @block.scalar
            def _(sc):
                for tt in range(8):
                    for e in range(NE):
                        for ht in range(32):
                            p.wait(sc, "sTm", mGU[(tt, e, ht, 0)])
                            if ht >= 2:
                                p.wait(sc, "sVt", mH[(tt, e, ht - 2)])
                            sc.activation(sg[ht % 2][:], ps_g[:], AF.Silu
                                          ).then_inc(p.sem["sP"], 1)

            @block.gpsimd
            def _(g):
                p.wait(g, "sD", gate[0])
                p.wait(g, "sD2", gate[1])
                p.wait(g, "sGD", gate[2])
                for tt in range(8):
                    for r in range(4):
                        i = tt * 4 + r
                        if i >= 1:
                            pr = divmod(i - 1, 4)
                            p.wait(g, "sV2", mXN[(pr[0], pr[1])])
                            g.dma_start(
                                out=ap(dst, (pr[0] * 512 + pr[1] * 128) * E, [[E, 128], [1, E]]),
                                in_=xnw[:]).then_inc(p.sem["sGD"], 16)
                        g.dma_start(out=xr[:],
                                    in_=ap(src, (tt * 512 + r * 128) * E, [[E, 128], [1, E]])
                                    ).then_inc(p.sem["sX"], 16)
                p.wait(g, "sV2", mXN[(7, 3)])
                g.dma_start(out=ap(dst, (7 * 512 + 3 * 128) * E, [[E, 128], [1, E]]),
                            in_=xnw[:]).then_inc(p.sem["sGD"], 16)

            @block.tensor
            def _(t):
                t.wait_ge(p.sem["sVt"], a_final_vt)
                t.wait_ge(p.sem["sV2"], a_final_v2)
                for tt in range(8):
                    p.wait(t, CRSEM(tt), mCR[tt])
                    for e in range(NE):
                        if tt > 0:
                            p.wait(t, "sV2", mAD[(tt - 1, 3, 6 + (e % 2))])
                        if e >= 2:
                            p.wait(t, "sV2", mCB_v[(tt, e - 2)])
                        t.matmul(ps_x[e % 2][:], onesb[:], crw[tt % 2][0:1, e * 512:(e + 1) * 512],
                                 start=True, stop=True).then_inc(p.sem["sTm"], 1)
                    for e in range(NE):
                        for ht in range(32):
                            j = (tt * NE + e) * 8 + ht // 4
                            p.wait(t, WGSEM(j), mWG[(tt, e, ht // 4)][1])
                            if ht >= 1:
                                p.wait(t, "sP", mSI[(tt, e, ht - 1)])
                            elif (tt, e) != (0, 0):
                                pe = (tt * NE + e - 1)
                                p.wait(t, "sP", mSI[(pe // NE, pe % NE, 31)])
                            last = None
                            for dc in range(NC):
                                last = t.matmul(ps_g[:],
                                                wgs[j % 2][:, ((ht % 4) * 8 + dc) * 128:((ht % 4) * 8 + dc + 1) * 128],
                                                xTf[:, dc * S + tt * 512:dc * S + (tt + 1) * 512],
                                                start=(dc == 0), stop=(dc == NC - 1))
                            last.then_inc(p.sem["sTm"], 1)
                            if ht >= 1:
                                p.wait(t, "sVt", mH[(tt, e, ht - 1)])
                            elif (tt, e) != (0, 0):
                                pe = (tt * NE + e - 1)
                                p.wait(t, "sVt", mH[(pe // NE, pe % NE, 31)])
                            last = None
                            for dc in range(NC):
                                last = t.matmul(ps_u[:],
                                                wus[j % 2][:, ((ht % 4) * 8 + dc) * 128:((ht % 4) * 8 + dc + 1) * 128],
                                                xTf[:, dc * S + tt * 512:dc * S + (tt + 1) * 512],
                                                start=(dc == 0), stop=(dc == NC - 1))
                            last.then_inc(p.sem["sTm"], 1)
                        for half in range(2):
                            if (e, half) != (0, 0):
                                prev = (tt, e, half - 1) if half == 1 else (tt, e - 1, 1)
                            elif tt > 0:
                                prev = (tt - 1, NE - 1, 1)
                            else:
                                prev = None
                            if prev is not None:
                                p.wait(t, "sV2", mOE[prev])
                            for g4 in range(8):
                                j, wv = mWD[(tt, e, half, g4)]
                                p.wait(t, WDSEM(j), wv)
                                last = None
                                for hti in range(4):
                                    ht = g4 * 4 + hti
                                    if half == 0 and g4 == 0 and hti == 0:
                                        p.wait(t, "sVt", mH[(tt, e, 31)])
                                    for d in range(4):
                                        last = t.matmul(ps_o[d][:],
                                                        wds[j % 2][:, hti * 512 + d * 128:hti * 512 + (d + 1) * 128],
                                                        h_sb[:, ht * 512:(ht + 1) * 512],
                                                        start=(ht == 0), stop=(ht == 31))
                                last.then_inc(p.sem["sPV"], 1)
                    p.wait(t, "sV2", mOE[(tt, NE - 1, 1)])
                    p.wait(t, "sOE", OAC(tt))
                    for r in range(4):
                        for c in range(8):
                            u = (tt * 4 + r) * 8 + c
                            if u >= 2:
                                pu = u - 2
                                p.wait(t, "sV2", mAD[(pu // 32, (pu % 32) // 8, pu % 8)])
                            t.transpose(ps_x[c % 2][:, 0:128],
                                        oacc[:, c * 512 + r * 128:c * 512 + (r + 1) * 128],
                                        idf[:]).then_inc(p.sem["sTt"], 1)

            @block.vector
            def _(v):
                for tt in range(8):
                    for e in range(NE):
                        p.wait(v, "sTm", mCB_t[(tt, e)])
                        v.tensor_copy(cbce[:, e * 512:(e + 1) * 512], ps_x[e % 2][:]
                                      ).then_inc(p.sem["sV2"], 1)
                    prev_add = [0]
                    for e in range(NE):
                        for ht in range(32):
                            if ht == 0:
                                if (tt, e) != (0, 0):
                                    pe = tt * NE + e - 1
                                    p.wait(v, "sPV", mDG[(pe // NE, pe % NE, 1, 7)])
                            p.wait(v, "sP", mSI[(tt, e, ht)])
                            p.wait(v, "sTm", mGU[(tt, e, ht, 1)])
                            v.tensor_mul(h_sb[:, ht * 512:(ht + 1) * 512], sg[ht % 2][:],
                                         ps_u[:]).then_inc(p.sem["sVt"], 1)
                        for half in range(2):
                            p.wait(v, "sPV", mDG[(tt, e, half, 7)])
                            if e == 0 and half == 0 and tt > 0:
                                p.wait(v, "sTt", mTR[(tt - 1, 3, 7)])
                            if e == 0:
                                last = None
                                for d in range(4):
                                    last = v.tensor_mul(oacc[:, (half * 4 + d) * 512:(half * 4 + d + 1) * 512],
                                                        ps_o[d][:], cbce[:, e * 512:(e + 1) * 512])
                                last.then_inc(p.sem["sV2"], 1)
                            else:
                                if prev_add[0]:
                                    p.wait(v, "sVc", prev_add[0])
                                for d in range(4):
                                    ins = v.tensor_mul(otmp[:, d * 512:(d + 1) * 512],
                                                       ps_o[d][:], cbce[:, e * 512:(e + 1) * 512])
                                ins.then_inc(p.sem["sV2"], 1)
                                p.wait(v, "sV2", mOE[(tt, e, half)])
                                last = None
                                for d in range(4):
                                    last = v.tensor_add(oacc[:, (half * 4 + d) * 512:(half * 4 + d + 1) * 512],
                                                        oacc[:, (half * 4 + d) * 512:(half * 4 + d + 1) * 512],
                                                        otmp[:, d * 512:(d + 1) * 512])
                                if (e, half) == (NE - 1, 1):
                                    last.then_inc(p.sem["sOE"], 1)
                                    prev_add[0] = 0
                                else:
                                    prev_add[0] = p.inc(last, "sVc")
                    for r in range(4):
                        p.wait(v, "sX", mXR[(tt, r)])
                        if tt * 4 + r >= 1:
                            pr = divmod(tt * 4 + r - 1, 4)
                            p.wait(v, "sGD", mST[(pr[0], pr[1])])
                        last = None
                        for c in range(8):
                            p.wait(v, "sTt", mTR[(tt, r, c)])
                            last = v.tensor_add(xnw[:, c * 128:(c + 1) * 128], ps_x[c % 2][:, 0:128],
                                                xr[:, c * 128:(c + 1) * 128])
                            last.then_inc(p.sem["sV2"], 1)




def emit_s3(nc, p, src, dst, wqkv_bf, wout_bf, bqkv_pp_in, bv_bc_in, bout_bc_in,
            masks_in, q3_hbm, vpad_d, idb, idf, onesb, epsc, cast_all, gate):
    with contextlib.ExitStack() as st:
        kpad = st.enter_context(nc.sbuf_tensor("s3_kpad", [128, NC * 4112], BF16))
        woutT = st.enter_context(nc.sbuf_tensor("s3_woT", [128, NC * E], BF16))
        bpp = st.enter_context(nc.sbuf_tensor("s3_bpp", [128, 16], F32))
        bvb = st.enter_context(nc.sbuf_tensor("s3_bvb", [128, E], F32))
        bob = st.enter_context(nc.sbuf_tensor("s3_bob", [128, E], F32))
        msk1 = st.enter_context(nc.sbuf_tensor("s3_m1", [128, 3 * 128], BF16))
        msk2 = st.enter_context(nc.sbuf_tensor("s3_m2", [16, 3 * 128], BF16))

        stb = st.enter_context(contextlib.ExitStack())
        wqkvT = stb.enter_context(nc.sbuf_tensor("s3_wqT", [128, NC * 3072], BF16))
        with contextlib.ExitStack() as s2, nc.Block() as block:
            wtmp = [s2.enter_context(nc.sbuf_tensor(f"s3_wt{i}", [128, E], BF16)) for i in range(2)]
            ps_w = [s2.enter_context(nc.psum_tensor(f"s3_pw{i}", [128, 128], BF16)) for i in range(2)]

            bWp = (p.cnt["sW"], p.cnt["sW2"]); bTt = p.cnt["sTt"]; bVt = p.cnt["sVt"]
            WSEM = lambda ob: ("sW", "sW2")[ob % 2]
            W_ = lambda ob: bWp[ob % 2] + 16 * (ob // 2 + 1)
            Tt = lambda u: bTt + u + 1
            Vt = lambda u: bVt + u + 1
            p.bump("sW", 16 * 16); p.bump("sW2", 16 * 16)
            cst_mark = p.bump("sCR", 16 * 5)
            p.bump("sTt", 32 * 8); p.bump("sVt", 32 * 8)

            @block.sync
            def _(sync):
                sync.wait_ge(p.sem["sTG"], cast_all[0])
                sync.wait_ge(p.sem["sC2"], cast_all[1])
                for ob in range(32):
                    w_bf = wqkv_bf if ob < 24 else wout_bf
                    row = ob * 128 if ob < 24 else (ob - 24) * 128
                    if ob >= 2:
                        p.wait(sync, "sVt", Vt((ob - 1) * 8))
                    sync.dma_start(out=wtmp[ob % 2][:],
                                   in_=ap(w_bf, row * E, [[E, 128], [1, E]])).then_inc(p.sem[WSEM(ob)], 16)
                sync.dma_start(out=bpp[:], in_=bqkv_pp_in).then_inc(p.sem["sCR"], 16)
                sync.dma_start(out=bvb[:], in_=bv_bc_in).then_inc(p.sem["sCR"], 16)
                sync.dma_start(out=bob[:], in_=bout_bc_in).then_inc(p.sem["sCR"], 16)
                sync.dma_start(out=msk1[:], in_=ap(masks_in, 0, [[128, 128], [144 * 128, 3], [1, 128]])
                               ).then_inc(p.sem["sCR"], 16)
                sync.dma_start(out=msk2[:], in_=ap(masks_in, 128 * 128, [[128, 16], [144 * 128, 3], [1, 128]])
                               ).then_inc(p.sem["sCR"], 16)

            @block.tensor
            def _(t):
                for ob in range(32):
                    p.wait(t, WSEM(ob), W_(ob))
                    for ic in range(8):
                        u = ob * 8 + ic
                        if u >= 2:
                            p.wait(t, "sVt", Vt(u - 2))
                        t.transpose(ps_w[u % 2][:], wtmp[ob % 2][:, ic * 128:(ic + 1) * 128],
                                    idb[:]).then_inc(p.sem["sTt"], 1)

            @block.vector
            def _(v):
                v.memset(kpad[:], 0.0)
                for ob in range(32):
                    for ic in range(8):
                        u = ob * 8 + ic
                        p.wait(v, "sTt", Tt(u))
                        if ob < 24:
                            dst_ap = wqkvT[:, ic * 3072 + ob * 128: ic * 3072 + (ob + 1) * 128]
                        else:
                            dst_ap = woutT[:, ic * E + (ob - 24) * 128: ic * E + (ob - 23) * 128]
                        v.tensor_copy(dst_ap, ps_w[u % 2][:]).then_inc(p.sem["sVt"], 1)

        with contextlib.ExitStack() as s2, nc.Block() as block:
            xt = [s2.enter_context(nc.sbuf_tensor(f"s3_xt{i}", [128, E], F32)) for i in range(2)]
            xcb = [s2.enter_context(nc.sbuf_tensor(f"s3_xc{i}", [128, E], F32)) for i in range(2)]
            xh = [s2.enter_context(nc.sbuf_tensor(f"s3_xh{i}", [128, E], BF16)) for i in range(2)]
            xT = [s2.enter_context(nc.sbuf_tensor(f"s3_xT{i}", [128, E], BF16)) for i in range(2)]
            qtmp = [s2.enter_context(nc.sbuf_tensor(f"s3_qt{i}", [128, NC * 128], BF16)) for i in range(2)]
            vtmp = [s2.enter_context(nc.sbuf_tensor(f"s3_vt{i}", [128, H * 65], BF16)) for i in range(2)]
            sqd = s2.enter_context(nc.sbuf_tensor("s3_sqd", [128, E], F32))
            sml = s2.enter_context(nc.sbuf_tensor("s3_sm", [128, 6], F32))
            ztile = s2.enter_context(nc.sbuf_tensor("s3_z", [128, H * 65], BF16))
            ps_tr = [s2.enter_context(nc.psum_tensor(f"s3_ptr{i}", [128, 128], BF16)) for i in range(2)]
            ps_q = [s2.enter_context(nc.psum_tensor(f"s3_pq{i}", [128, 512], F32)) for i in range(2)]
            ps_k = [s2.enter_context(nc.psum_tensor(f"s3_pk{i}", [128, 512], F32)) for i in range(2)]
            ps_v = [s2.enter_context(nc.psum_tensor(f"s3_pv{i}", [128, 512], F32)) for i in range(2)]

            bXp = (p.cnt["sX"], p.cnt["sX2"]); bV = p.cnt["sV"]; bS = p.cnt["sS"]
            bTt = p.cnt["sTt"]; bVt = p.cnt["sVt"]; bTm = p.cnt["sTm"]
            bE = p.cnt["sV2"]; bKQp = (p.cnt["sKQ"], p.cnt["sKQ2"])
            bDp = (p.cnt["sD"], p.cnt["sD2"])
            XSEM = lambda b: ("sX", "sX2")[b % 2]
            X_ = lambda b: bXp[b % 2] + 16 * (b // 2 + 1)
            V_ = lambda b, i: bV + 2 * b + i
            S_ = lambda b, i: bS + 2 * b + i
            Tt = lambda g: bTt + g + 1
            Vt = lambda g: bVt + g + 1
            Tm = lambda b, i: bTm + 3 * b + i
            E_ = lambda b, i: bE + 3 * b + i
            KQSEM = lambda b: ("sKQ", "sKQ2")[b % 2]
            KQ = lambda b: bKQp[b % 2] + 16 * (b // 2 + 1)
            DSEM = lambda b: ("sD", "sD2")[b % 2]
            D_ = lambda b: bDp[b % 2] + 16 * (b // 2 + 1)
            p.bump("sX", 16 * (NBLK // 2)); p.bump("sX2", 16 * (NBLK // 2))
            p.bump("sV", 2 * NBLK); p.bump("sS", 2 * NBLK)
            p.bump("sTt", 8 * NBLK); p.bump("sVt", 8 * NBLK); p.bump("sTm", 3 * NBLK)
            p.bump("sV2", 3 * NBLK)
            p.bump("sKQ", 16 * (NBLK // 2)); p.bump("sKQ2", 16 * (NBLK // 2))
            p.bump("sD", 16 * (NBLK // 2)); p.bump("sD2", 16 * (NBLK // 2))
            zt_mark = p.bump("sLx", 1)

            @block.gpsimd
            def _(g):
                p.wait(g, "sD", gate[0])
                p.wait(g, "sD2", gate[1])
                p.wait(g, "sGD", gate[2])
                for b in range(NBLK):
                    if b >= 2:
                        p.wait(g, "sS", S_(b - 2, 1))
                    g.dma_start(out=xt[b % 2][:],
                                in_=ap(src, b * 128 * E, [[E, 128], [1, E]])).then_inc(p.sem[XSEM(b)], 16)

            def s3_kqv_evacs(v, b):
                bb = b % 2
                p.wait(v, "sTm", Tm(b, 1))
                if b >= 2:
                    p.wait(v, KQSEM(b - 2), KQ(b - 2))
                last = None
                for bank in range(2):
                    for hp4 in range(4):
                        hp = bank * 4 + hp4
                        last = v.tensor_scalar(qtmp[bb][:, hp * 128:(hp + 1) * 128],
                                               ps_q[bank][:, hp4 * 128:(hp4 + 1) * 128],
                                               bpp[:, hp:hp + 1], None, ALU.add)
                last.then_inc(p.sem["sV2"], 1)
                p.wait(v, "sTm", Tm(b, 2))
                for bank in range(2):
                    for hp4 in range(4):
                        hp = bank * 4 + hp4
                        last = v.tensor_scalar(
                            kpad[:, hp * 4112 + 8 + b * 128: hp * 4112 + 8 + (b + 1) * 128],
                            ps_k[bank][:, hp4 * 128:(hp4 + 1) * 128],
                            bpp[:, 8 + hp:9 + hp], None, ALU.add)
                last.then_inc(p.sem["sV2"], 1)
                p.wait(v, "sTm", Tm(b, 3))
                if b >= 2:
                    p.wait(v, DSEM(b - 2), D_(b - 2))
                for half in range(2):
                    dst_ap = ap(vtmp[bb], half * 8 * 65, [[H * 65, 128], [65, 8], [1, 64]])
                    last = v.scalar_tensor_tensor(dst_ap, ps_v[half][:], 1.0,
                                                  bvb[:, half * 512:(half + 1) * 512],
                                                  ALU.mult, ALU.add)
                last.then_inc(p.sem["sV2"], 1)

            @block.vector
            def _(v):
                    p.wait(v, "sCR", cst_mark)
                    v.memset(vtmp[0][:], 1.0)
                    v.memset(vtmp[1][:], 1.0)
                    v.memset(ztile[:], 0.0).then_inc(p.sem["sLx"], 1)
                    for b in range(NBLK):
                        bb = b % 2
                        p.wait(v, XSEM(b), X_(b))
                        if b >= 2:
                            p.wait(v, "sS", S_(b - 2, 1))
                        ln_vec_pre(p, v, xt[bb][:], sml[:, bb:bb + 1], xcb[bb][:], "sV")
                        p.wait(v, "sS", S_(b, 2))
                        if b >= 2:
                            p.wait(v, "sTt", Tt((b - 2) * 8 + 7))
                        v.tensor_scalar(xh[bb][:], xcb[bb][:], sml[:, 4 + bb:5 + bb], None,
                                        ALU.mult).then_inc(p.sem["sV"], 1)
                        for c in range(NC):
                            g = b * 8 + c
                            p.wait(v, "sTt", Tt(g))
                            v.tensor_copy(xT[bb][:, c * 128:(c + 1) * 128],
                                          ps_tr[c % 2][:]).then_inc(p.sem["sVt"], 1)
                        if b >= 1:
                            s3_kqv_evacs(v, b - 1)
                    s3_kqv_evacs(v, NBLK - 1)

            @block.scalar
            def _(sc):
                for b in range(NBLK):
                    bb = b % 2
                    p.wait(sc, "sV", V_(b, 1))
                    if b >= 2:
                        p.wait(sc, "sV", V_(b - 2, 2))
                    ln_scalar(p, sc, xcb[bb][:], sqd[:], sml[:, 2 + bb:3 + bb],
                              sml[:, 4 + bb:5 + bb], epsc[:, 0:1], "sS", S_(b, 1))

            @block.tensor
            def _(t):
                for b in range(NBLK):
                    bb = b % 2
                    p.wait(t, "sV", V_(b, 2))
                    for c in range(NC):
                        g = b * 8 + c
                        if g >= 2:
                            p.wait(t, "sVt", Vt(g - 2))
                        t.transpose(ps_tr[c % 2][:], xh[bb][:, c * 128:(c + 1) * 128],
                                    idb[:]).then_inc(p.sem["sTt"], 1)
                    p.wait(t, "sVt", Vt(b * 8 + 7))
                    if b >= 1:
                        p.wait(t, "sV2", E_(b - 1, 3))
                    for qk, psb in ((0, ps_q), (1, ps_k)):
                        last = None
                        for bank in range(2):
                            for hp4 in range(4):
                                hp = bank * 4 + hp4
                                for c in range(NC):
                                    last = t.matmul(
                                        psb[bank][:, hp4 * 128:(hp4 + 1) * 128],
                                        wqkvT[:, c * 3072 + qk * 1024 + hp * 128:
                                              c * 3072 + qk * 1024 + (hp + 1) * 128],
                                        xT[bb][:, c * 128:(c + 1) * 128],
                                        start=(hp4 == 0 and c == 0), stop=(hp4 == 3 and c == NC - 1))
                        last.then_inc(p.sem["sTm"], 1)
                    last = None
                    for half in range(2):
                        for c in range(NC):
                            last = t.matmul(ps_v[half][:],
                                            xT[bb][:, c * 128:(c + 1) * 128],
                                            wqkvT[:, c * 3072 + 2048 + half * 512:
                                                  c * 3072 + 2048 + (half + 1) * 512],
                                            start=(c == 0), stop=(c == NC - 1))
                    last.then_inc(p.sem["sTm"], 1)


            @block.sync
            def _(sync):
                for b in range(NBLK):
                    p.wait(sync, "sV2", E_(b, 1))
                    sync.dma_start(out=ap(q3_hbm, b * 128, [[S, 128], [128 * S, NC], [1, 128]]),
                                   in_=qtmp[b % 2][:]).then_inc(p.sem[KQSEM(b)], 16)
                    p.wait(sync, "sV2", E_(b, 3))
                    sync.dma_start(out=ap(vpad_d, (8 + b * 128) * (H * 65), [[H * 65, 128], [1, H * 65]]),
                                   in_=vtmp[b % 2][:]).then_inc(p.sem[DSEM(b)], 16)
                sync.wait_ge(p.sem["sLx"], zt_mark)
                sync.dma_start(out=ap(vpad_d, 0, [[H * 65, 8], [1, H * 65]]),
                               in_=ztile[0:8, :]).then_inc(p.sem["sCR"], 16)
                sync.dma_start(out=ap(vpad_d, 4104 * (H * 65), [[H * 65, 120], [1, H * 65]]),
                               in_=ztile[0:120, :]).then_inc(p.sem["sCR"], 16)
            p.bump("sCR", 32)
            vpd_done = (p.cnt["sD"], p.cnt["sD2"], p.cnt["sKQ"], p.cnt["sKQ2"], p.cnt["sCR"])

        stb.close()
        with contextlib.ExitStack() as s2, nc.Block() as block:
            vpad = s2.enter_context(nc.sbuf_tensor("s3_vpad", [128, 33 * (H * 65)], BF16))
            qts = [s2.enter_context(nc.sbuf_tensor(f"s3_qts{i}", [128, NC * 128], BF16)) for i in range(2)]
            pt1 = [s2.enter_context(nc.sbuf_tensor(f"s3_pt1{i}", [128, 128], BF16)) for i in range(2)]
            pt2 = [s2.enter_context(nc.sbuf_tensor(f"s3_pt2{i}", [16, 128], BF16)) for i in range(2)]
            rsb = s2.enter_context(nc.sbuf_tensor("s3_rsb", [1, H * 128], F32))
            rcb = s2.enter_context(nc.sbuf_tensor("s3_rcb", [1, H * 128], BF16))
            rcs = s2.enter_context(nc.sbuf_tensor("s3_rcs", [64, H * 128], F32))
            ctxT = s2.enter_context(nc.sbuf_tensor("s3_ctxT", [128, NC * 128], BF16))
            yb = s2.enter_context(nc.sbuf_tensor("s3_yb", [128, E], F32))
            xr = s2.enter_context(nc.sbuf_tensor("s3_xr", [128, E], F32))
            xnw = s2.enter_context(nc.sbuf_tensor("s3_xnw", [128, E], F32))
            ps_sA = [s2.enter_context(nc.psum_tensor(f"s3_psA{i}", [128, 512], F32)) for i in range(2)]
            ps_ob = [s2.enter_context(nc.psum_tensor(f"s3_pob{i}", [65, 512], F32)) for i in range(4)]
            ps_y = [s2.enter_context(nc.psum_tensor(f"s3_py{i}", [128, 512], F32)) for i in range(2)]

            bKQp = (p.cnt["sKQ"], p.cnt["sKQ2"])
            bST = p.cnt["sST"]; bP = p.cnt["sP"]; bPV = p.cnt["sPV"]
            bOE = p.cnt["sOE"]; bTm = p.cnt["sTm"]; bE = p.cnt["sV2"]; bX = p.cnt["sX"]
            bD = p.cnt["sGD"]; bV = p.cnt["sV"]
            KQSEM = lambda qt: ("sKQ", "sKQ2")[qt % 2]
            KQ = lambda qt: bKQp[qt % 2] + 16 * (qt // 2 + 1)
            ST = lambda n: bST + n + 1
            p.bump("sKQ", 16 * 16); p.bump("sKQ2", 16 * 16)
            p.bump("sST", 32 * 16); p.bump("sP", 32 * 16)
            p.bump("sPV", 32 * 16)
            MSK = lambda n: bOE + (n // 16) + n + 1
            RCR = lambda qt: bOE + 17 * (qt + 1)
            p.bump("sOE", 17 * 32)
            v2c = bE
            mRC = {}; mCTX = {}
            for qt in range(32):
                for j in range(4):
                    mRC[(qt, j)] = v2c = v2c + 1
                for hh in range(16):
                    mCTX[(qt, hh)] = v2c = v2c + 1
            p.bump("sV2", v2c - bE)
            EXP = lambda n: bP + n + 1
            PVm = lambda n: bPV + n + 1
            tmc = bTm
            mRCT = {}; mY = {}
            for qt in range(32):
                for j in range(4):
                    mRCT[(qt, j)] = tmc = tmc + 1
                for nh in range(2):
                    mY[(qt, nh)] = tmc = tmc + 1
            p.bump("sTm", tmc - bTm)
            XR = lambda qt: bX + 16 * (qt + 1)
            D_ = lambda qt: bD + 16 * (qt + 1)
            XN = lambda qt: bV + qt + 1
            p.bump("sX", 16 * 32); p.bump("sGD", 16 * 32); p.bump("sV", 32)
            vp_mark = p.bump("sW", 16)

            @block.sync
            def _(sync):
                sync.wait_ge(p.sem["sD"], vpd_done[0])
                sync.wait_ge(p.sem["sD2"], vpd_done[1])
                sync.wait_ge(p.sem["sKQ"], vpd_done[2])
                sync.wait_ge(p.sem["sKQ2"], vpd_done[3])
                sync.wait_ge(p.sem["sCR"], vpd_done[4])
                sync.dma_start(out=vpad[:],
                               in_=ap(vpad_d, 0, [[H * 65, 128], [128 * H * 65, 33], [1, H * 65]])
                               ).then_inc(p.sem["sW"], 16)
                for qt in range(32):
                    if qt >= 2:
                        p.wait(sync, "sST", ST(16 * (qt - 1) - 1))
                    sync.dma_start(out=qts[qt % 2][:],
                                   in_=ap(q3_hbm, qt * 128, [[S, 128], [128 * S, NC], [1, 128]])
                                   ).then_inc(p.sem[KQSEM(qt)], 16)

            @block.gpsimd
            def _(g):
                p.wait(g, "sD", gate[0])
                p.wait(g, "sD2", gate[1])
                p.wait(g, "sGD", gate[2])
                for qt in range(32):
                    if qt >= 1:
                        p.wait(g, "sV", XN(qt - 1))
                        g.dma_start(out=ap(dst, (qt - 1) * 128 * E, [[E, 128], [1, E]]),
                                    in_=xnw[:]).then_inc(p.sem["sGD"], 16)
                    g.dma_start(out=xr[:],
                                in_=ap(src, qt * 128 * E, [[E, 128], [1, E]])).then_inc(p.sem["sX"], 16)
                p.wait(g, "sV", XN(31))
                g.dma_start(out=ap(dst, 31 * 128 * E, [[E, 128], [1, E]]),
                            in_=xnw[:]).then_inc(p.sem["sGD"], 16)

            @block.tensor
            def _(t):
                t.wait_ge(p.sem["sW"], vp_mark)
                for qt in range(32):
                    p.wait(t, KQSEM(qt), KQ(qt))

                    def s3_s(hh):
                        n = qt * 16 + hh
                        hp = hh // 2; hq = (hh % 2) * 64
                        if n >= 2:
                            p.wait(t, "sP", EXP(n - 2))
                        t.matmul(ps_sA[n % 2][:, 0:128],
                                 kpad[0:64, hp * 4112 + qt * 128: hp * 4112 + (qt + 1) * 128]
                                 if hq == 0 else
                                 kpad[64:128, hp * 4112 + qt * 128: hp * 4112 + (qt + 1) * 128],
                                 qts[qt % 2][hq:hq + 64, hp * 128:(hp + 1) * 128],
                                 start=True, stop=True)
                        t.matmul(ps_sA[n % 2][0:16, 128:256],
                                 kpad[0:64, hp * 4112 + (qt + 1) * 128: hp * 4112 + (qt + 1) * 128 + 16]
                                 if hq == 0 else
                                 kpad[64:128, hp * 4112 + (qt + 1) * 128: hp * 4112 + (qt + 1) * 128 + 16],
                                 qts[qt % 2][hq:hq + 64, hp * 128:(hp + 1) * 128],
                                 start=False, stop=True,
                                 skip_group_check=True).then_inc(p.sem["sST"], 1)

                    def s3_pv(hh):
                        n = qt * 16 + hh
                        if hh % 4 == 0 and qt > 0:
                            p.wait(t, "sV2", mCTX[(qt - 1, 15)])
                        p.wait(t, "sOE", MSK(n))
                        t.matmul(ps_ob[hh // 4][:, (hh % 4) * 128:(hh % 4 + 1) * 128],
                                 vpad[:, qt * (H * 65) + hh * 65: qt * (H * 65) + hh * 65 + 65],
                                 pt1[n % 2][:],
                                 start=(hh % 4 == 0), stop=True, skip_group_check=True)
                        t.matmul(ps_ob[hh // 4][:, (hh % 4) * 128:(hh % 4 + 1) * 128],
                                 vpad[0:16, (qt + 1) * (H * 65) + hh * 65: (qt + 1) * (H * 65) + hh * 65 + 65],
                                 pt2[n % 2][:],
                                 start=False, stop=True,
                                 skip_group_check=True).then_inc(p.sem["sPV"], 1)

                    for hh in range(16):
                        s3_s(hh)
                        if hh >= 1:
                            s3_pv(hh - 1)
                    s3_pv(15)
                    p.wait(t, "sOE", RCR(qt))
                    for j in range(4):
                        if j < 2:
                            if qt > 0:
                                p.wait(t, "sV", XN(qt - 1))
                        else:
                            p.wait(t, "sV2", mRC[(qt, j - 2)])
                        t.matmul(ps_y[j % 2][0:64, :], onesb[0:1, 0:64], rcb[0:1, j * 512:(j + 1) * 512],
                                 start=True, stop=True).then_inc(p.sem["sTm"], 1)
                    p.wait(t, "sV2", mCTX[(qt, 15)])
                    for nh in range(2):
                        p.wait(t, "sV2", mRC[(qt, 2 + nh)])
                        last = None
                        for cc in range(8):
                            last = t.matmul(ps_y[nh][:],
                                            ctxT[:, cc * 128:(cc + 1) * 128],
                                            woutT[:, cc * E + nh * 512: cc * E + (nh + 1) * 512],
                                            start=(cc == 0), stop=(cc == 7))
                        last.then_inc(p.sem["sTm"], 1)

            @block.scalar
            def _(sc):
                for qt in range(32):
                    for hh in range(16):
                        n = qt * 16 + hh
                        p.wait(sc, "sST", ST(n))
                        if n >= 2:
                            p.wait(sc, "sPV", PVm(n - 2))
                        sc.activation(pt1[n % 2][:], ps_sA[n % 2][:, 0:128], AF.Exp,
                                      scale=1.0 / 8.0)
                        sc.activation(pt2[n % 2][:], ps_sA[n % 2][0:16, 128:256], AF.Exp,
                                      scale=1.0 / 8.0).then_inc(p.sem["sP"], 1)

            @block.vector
            def _(v):
                for qt in range(32):
                    cls = 0 if qt == 0 else (2 if qt == 31 else 1)
                    for hh in range(16):
                        n = qt * 16 + hh
                        p.wait(v, "sP", EXP(n))
                        v.tensor_mul(pt1[n % 2][:], pt1[n % 2][:],
                                     msk1[:, cls * 128:(cls + 1) * 128])
                        v.tensor_mul(pt2[n % 2][:], pt2[n % 2][:],
                                     msk2[:, cls * 128:(cls + 1) * 128]).then_inc(p.sem["sOE"], 1)
                    for hh in range(16):
                        p.wait(v, "sPV", PVm(qt * 16 + hh))
                        ins = v.tensor_copy(rsb[:, hh * 128:(hh + 1) * 128],
                                            ps_ob[hh // 4][64:65, (hh % 4) * 128:(hh % 4 + 1) * 128])
                    cc = p.inc(ins, "sVc")
                    p.wait(v, "sVc", cc)
                    cc = p.inc(v.reciprocal(rsb[:], rsb[:]), "sVc")
                    p.wait(v, "sVc", cc)
                    v.tensor_copy(rcb[:], rsb[:]).then_inc(p.sem["sOE"], 1)
                    for j in range(4):
                        p.wait(v, "sTm", mRCT[(qt, j)])
                        v.tensor_copy(rcs[:, j * 512:(j + 1) * 512],
                                      ps_y[j % 2][0:64, :]).then_inc(p.sem["sV2"], 1)
                    p.wait(v, "sV2", mRC[(qt, 3)])
                    for hh in range(16):
                        hp = hh // 2; hq = (hh % 2) * 64
                        v.tensor_mul(ctxT[hq:hq + 64, hp * 128:(hp + 1) * 128],
                                     ps_ob[hh // 4][0:64, (hh % 4) * 128:(hh % 4 + 1) * 128],
                                     rcs[:, hh * 128:(hh + 1) * 128]).then_inc(p.sem["sV2"], 1)
                    p.wait(v, "sX", XR(qt))
                    if qt >= 1:
                        p.wait(v, "sGD", D_(qt - 1))
                    for nh in range(2):
                        p.wait(v, "sTm", mY[(qt, nh)])
                        cc = p.inc(v.tensor_add(yb[:, nh * 512:(nh + 1) * 512], ps_y[nh][:],
                                                bob[:, nh * 512:(nh + 1) * 512]), "sVc")
                    p.wait(v, "sVc", cc)
                    v.tensor_add(xnw[:], yb[:], xr[:]).then_inc(p.sem["sV"], 1)




def build(nc):
    def inp(name, shape, dt=F32):
        return nc.dram_tensor(name, list(shape), dt, kind="ExternalInput").ap()

    x_in = inp("x", [S, E])
    wq_in = inp("wq", [E, E]); wk_in = inp("wk", [E, E]); wv_in = inp("wv", [E, E])
    tg_in = inp("tg", [8, 256, S])
    gamma_bc = inp("gamma_bc", [128, E]); beta_bc = inp("beta_bc", [128, E])
    wqkv_in = inp("wqkv", [3 * E, E]); bqkv_pp = inp("bqkv_pp", [128, 16]); bv_bc = inp("bv_bc", [128, E])
    wout_in = inp("wout", [E, E]); bout_bc = inp("bout_bc", [128, E])
    wr_in = inp("wr", [E, NE])
    wg_in = inp("wg", [NE * E * HID]); wu_in = inp("wu", [NE * E * HID])
    wd_in = inp("wdn", [NE, HID, E])
    masks_in = inp("masks", [3, 144, 128], BF16)
    idf_in = inp("idf", [128, 128]); idb_in = inp("idb", [128, 128], BF16)
    out_t = nc.dram_tensor("out", [S, E], F32, kind="ExternalOutput").ap()

    def scratch(name, shape, dt):
        return nc.dram_tensor(name, list(shape), dt, kind="Internal").ap()

    resid = scratch("resid", [S, E], F32)
    wq_bf = scratch("wq_bf", [E, E], BF16)
    wk_bf = scratch("wk_bf", [E, E], BF16)
    wv_bf = scratch("wv_bf", [E, E], BF16)
    tg_bf = scratch("tg_bf", [8 * 256 * S], BF16)
    wgu_bf = scratch("wgu_bf", [2 * NE * 32 * 128 * 8 * 128], BF16)
    wd_bf = scratch("wd_bf", [NE * HID * E], BF16)
    wqkv_bf = scratch("wqkv_bf", [3 * E, E], BF16)
    wout_bf = scratch("wout_bf", [E, E], BF16)
    k1_hbm = scratch("k1_hbm", [NC * 128 * S], BF16)
    q3_hbm = scratch("q3_hbm", [NC * 128 * S], BF16)
    vpad_d = scratch("vpad_d", [33 * 128 * H * 65], BF16)
    ct_dram = scratch("ct_dram", [NE, S], BF16)

    p = P(nc)
    p.mksem("sC", "sX", "sW", "sKQ", "sTt", "sVt", "sTm", "sV2", "sV", "sS",
            "sP", "sPV", "sST", "sOE", "sLx", "sLr", "sVc", "sSc", "sD",
            "sC2", "sX2", "sW2", "sKQ2", "sD2", "sCR", "sCR2", "sGD", "sTG", "sZB")

    with p.ctx, contextlib.ExitStack() as stg:
        idb = stg.enter_context(nc.sbuf_tensor("g_idb", [128, 128], BF16))
        idf = stg.enter_context(nc.sbuf_tensor("g_idf", [128, 128], F32))
        onesb = stg.enter_context(nc.sbuf_tensor("g_ones", [1, 128], BF16))
        epsc = stg.enter_context(nc.sbuf_tensor("g_eps", [128, 2], F32))

        with nc.Block() as block:
            @block.sync
            def _(sync):
                sync.dma_start(out=idb[:], in_=idb_in).then_inc(p.sem["sW"], 16)
                sync.dma_start(out=idf[:], in_=idf_in).then_inc(p.sem["sW"], 16)
            p.bump("sW", 32)
            p.aux_constw = p.cnt["sW"]

            @block.vector
            def _(v):
                v.memset(onesb[:], 1.0)
                v.memset(epsc[:, 0:1], 1e-5)
                v.memset(epsc[:, 1:2], 1e-3)

            @block.gpsimd
            def _(g):
                p.inc(g.dma_start(out=ap(tg_bf, 0, [[1, 8 * 256 * S]]),
                                  in_=ap(tg_in, 0, [[1, 8 * 256 * S]])), "sTG", 16)
                for mi, w_src in ((0, wg_in), (1, wu_in)):
                    for e in range(NE):
                        src = ap(w_src, e * E * HID, [[1, E * HID]])
                        dst = ap(wgu_bf, (mi * NE + e) * E * HID, [[1, E * HID]])
                        p.inc(g.dma_start(out=dst, in_=src), "sC2", 16)
                for e in range(NE):
                    for qtr in range(4):
                        p.inc(g.dma_start(out=ap(wd_bf, (e * HID + qtr * 1024) * E, [[1, 1024 * E]]),
                                          in_=ap(wd_in, (e * HID + qtr * 1024) * E, [[1, 1024 * E]])),
                              "sC2", 16)
                p.inc(g.dma_start(out=wqkv_bf, in_=wqkv_in), "sC2", 16)
                p.inc(g.dma_start(out=wout_bf, in_=wout_in), "sC2", 16)
        cast1 = p.cnt["sTG"]
        cast_all = (cast1, p.cnt["sC2"])

        s1_dst = resid if N_PHASES > 1 else out_t
        emit_s1(nc, p, x_in, s1_dst, wq_in, wk_in, wv_in, tg_bf, gamma_bc, beta_bc,
                k1_hbm, idb, idf, onesb, epsc, cast1)
        g1 = (p.cnt["sD"], p.cnt["sD2"], p.cnt["sGD"])
        if N_PHASES >= 2:
            s2_dst = resid if N_PHASES > 2 else out_t
            emit_moe(nc, p, resid, s2_dst, wgu_bf, wd_bf, wr_in, ct_dram, idb, idf, onesb, epsc, cast_all, "m1", g1)
        g2 = (p.cnt["sD"], p.cnt["sD2"], p.cnt["sGD"])
        if N_PHASES >= 3:
            s3_dst = resid if N_PHASES > 3 else out_t
            emit_s3(nc, p, resid, s3_dst, wqkv_bf, wout_bf, bqkv_pp, bv_bc, bout_bc,
                    masks_in, q3_hbm, vpad_d, idb, idf, onesb, epsc, cast_all, g2)
        g3 = (p.cnt["sD"], p.cnt["sD2"], p.cnt["sGD"])
        if N_PHASES >= 4:
            emit_moe(nc, p, resid, out_t, wgu_bf, wd_bf, wr_in, ct_dram, idb, idf, onesb, epsc, cast_all, "m2", g3)

        with nc.Block() as block:
            fin = p.cnt["sD"]
            fin2 = p.cnt["sD2"]
            fin3 = p.cnt["sGD"]

            @block.sync
            def _(sync):
                sync.wait_ge(p.sem["sD"], fin)
                sync.wait_ge(p.sem["sD2"], fin2)
                sync.wait_ge(p.sem["sGD"], fin3)
    return nc


def _host_inputs(inputs):
    import ml_dtypes
    rel = np.asarray(inputs["L0_attn_a_rel_table"], np.float32)
    qd = np.arange(256)[:, None]
    jp = np.arange(256)[None, :]
    tgs = []
    for g in range(8, 16):
        rows = np.minimum(4095 - 256 * g - qd + jp, 2046)
        tgs.append(rel[rows].reshape(256, S))
    tg = np.ascontiguousarray(np.stack(tgs)).astype(np.float32)

    bqkv = np.asarray(inputs["L0_attn_b_b_qkv"], np.float32)
    import ml_dtypes
    f = np.arange(144)[:, None]
    pcol = np.arange(128)[None, :]
    band = (f - pcol >= 0) & (f - pcol <= 16)
    m_mid = np.where(band, 1.0, 0.0).astype(np.float32)
    m_first = np.where(band & (f >= 8), 1.0, 0.0).astype(np.float32)
    m_last = np.where(band & (f <= 135), 1.0, 0.0).astype(np.float32)
    masks = np.stack([m_first, m_mid, m_last]).astype(ml_dtypes.bfloat16)

    def bc(v):
        return np.ascontiguousarray(np.broadcast_to(np.asarray(v, np.float32), (128, E)))

    common = {
        "wq": np.asarray(inputs["L0_attn_a_Wq"], np.float32),
        "wk": np.asarray(inputs["L0_attn_a_Wk"], np.float32),
        "wv": np.asarray(inputs["L0_attn_a_Wv"], np.float32),
        "tg": tg,
        "gamma_bc": bc(inputs["L0_attn_a_gamma"]),
        "beta_bc": bc(inputs["L0_attn_a_beta"]),
        "wqkv": np.asarray(inputs["L0_attn_b_w_qkv"], np.float32),
        "bqkv_pp": np.ascontiguousarray(bqkv[0:2048].reshape(16, 128).T),
        "bv_bc": bc(bqkv[2048:3072]),
        "wout": np.asarray(inputs["L0_attn_b_w_out"], np.float32),
        "bout_bc": bc(inputs["L0_attn_b_b_out"]),
        "wr": np.asarray(inputs["L0_ffn_Wr"], np.float32),
        "wg": np.ascontiguousarray(
            np.asarray(inputs["L0_ffn_Wg"], np.float32)
            .reshape(NE, 8, 128, 32, 128).transpose(0, 3, 2, 1, 4)).ravel(),
        "wu": np.ascontiguousarray(
            np.asarray(inputs["L0_ffn_Wu"], np.float32)
            .reshape(NE, 8, 128, 32, 128).transpose(0, 3, 2, 1, 4)).ravel(),
        "wdn": np.asarray(inputs["L0_ffn_Wd"], np.float32),
        "masks": masks,
        "idf": np.eye(128, dtype=np.float32),
        "idb": np.eye(128, dtype=np.float32).astype(ml_dtypes.bfloat16),
    }
    x = np.asarray(inputs["L0_attn_a_x"], np.float32)
    in_maps = []
    for c in range(8):
        m = dict(common)
        m["x"] = np.ascontiguousarray(x[c])
        in_maps.append(m)
    return in_maps


def kernel(**inputs):
    in_maps = _host_inputs(inputs)
    nc = bass.Bass("TRN2", target_bir_lowering=False)
    build(nc)
    res = run_bass_kernel_spmd(nc, in_maps, core_ids=list(range(8)))
    return np.stack([np.asarray(r["out"], np.float32) for r in res.results])


# version 07
# speedup vs baseline: 1.0542x; 1.0542x over previous
import math
import numpy as np
import contextlib
import concourse.bass as bass
import concourse.mybir as mybir
from concourse.bass_utils import run_bass_kernel_spmd

F32 = mybir.dt.float32
BF16 = mybir.dt.bfloat16
AF = mybir.ActivationFunctionType
ALU = mybir.AluOpType
AX = mybir.AxisListType.X

S, E, H, HD = 4096, 1024, 16, 64
HID, NE = 4096, 8
NBLK = S // 128
NC = E // 128
DS = 1.0 / math.sqrt(12.0)
NEG = -1.0e30
PAIRS = [(qm + g, qm, g) for g in range(8, 16) for qm in range(0, 16 - g)]
PAIR_IDX = {(hh, qm): i for i, (hh, qm, g) in enumerate(PAIRS)}

N_PHASES = 4


class P:
    def __init__(self, nc):
        self.nc = nc
        self.sem = {}
        self.cnt = {}
        self.ctx = contextlib.ExitStack()

    def mksem(self, *names):
        for n in names:
            self.sem[n] = self.ctx.enter_context(self.nc.semaphore(n))
            self.cnt[n] = 0

    def inc(self, instr, name, n=1):
        instr.then_inc(self.sem[name], n)
        self.cnt[name] += n
        return self.cnt[name]

    def bump(self, name, n):
        self.cnt[name] += n
        return self.cnt[name]

    def wait(self, eng, name, val):
        if val > 0:
            eng.wait_ge(self.sem[name], val)

    def check(self, name, val):
        assert self.cnt[name] == val, (name, self.cnt[name], val)


def ap(t, offset, dims):
    return bass.AP(tensor=t.tensor if hasattr(t, "tensor") else t, offset=offset,
                   ap=[list(d) for d in dims])


def ln_vec_pre(p, v, x_ap, mu, xc_out, mark):
    n = x_ap.shape[-1]
    c = p.inc(v.reduce_sum(mu, x_ap, axis=AX), "sVc")
    p.wait(v, "sVc", c)
    c = p.inc(v.tensor_scalar_mul(mu, mu, 1.0 / n), "sVc")
    p.wait(v, "sVc", c)
    v.tensor_scalar(xc_out, x_ap, mu, None, ALU.subtract).then_inc(p.sem[mark], 1)


def ln_scalar(p, sc, xc_ap, dump_ap, var, rs, eps_ap, mark, v1):
    n = xc_ap.shape[-1]
    c0 = p.inc(sc.activation(dump_ap, xc_ap, AF.Square, accum_out=var), "sSc")
    p.wait(sc, "sSc", c0)
    sc.activation(rs, var, AF.Ln, scale=1.0 / n, bias=eps_ap).then_inc(p.sem[mark], 1)
    p.wait(sc, mark, v1)
    sc.activation(rs, rs, AF.Exp, scale=-0.5).then_inc(p.sem[mark], 1)


def emit_s1(nc, p, src, dst, wq_in, wk_in, wv_in, tg_bf, gamma_bc_in, beta_bc_in,
            k1_hbm, idb, idf, onesb, epsc, cast1):
  with contextlib.ExitStack() as st:
    vaug = st.enter_context(nc.sbuf_tensor("s1_vaug", [128, NBLK * (H * 65)], BF16))
    q_sb = st.enter_context(nc.sbuf_tensor("s1_q", [128, NC * S], BF16))

    with contextlib.ExitStack() as s2, nc.Block() as block:
        wqs = s2.enter_context(nc.sbuf_tensor("s1_wq", [128, NC * E], BF16))
        wks = s2.enter_context(nc.sbuf_tensor("s1_wk", [128, NC * E], BF16))
        wvs = s2.enter_context(nc.sbuf_tensor("s1_wv", [128, NC * E], BF16))
        xt = [s2.enter_context(nc.sbuf_tensor(f"s1_xt{i}", [128, E], F32)) for i in range(2)]
        xcb = [s2.enter_context(nc.sbuf_tensor(f"s1_xc{i}", [128, E], F32)) for i in range(2)]
        xh = [s2.enter_context(nc.sbuf_tensor(f"s1_xh{i}", [128, E], BF16)) for i in range(2)]
        xT = [s2.enter_context(nc.sbuf_tensor(f"s1_xT{i}", [128, E], BF16)) for i in range(2)]
        ktmp = [s2.enter_context(nc.sbuf_tensor("s1_ktm0", [128, NC * 128], BF16))] * 2
        sml = s2.enter_context(nc.sbuf_tensor("s1_sm", [128, 6], F32))
        ps_tr = [s2.enter_context(nc.psum_tensor(f"s1_ptr{i}", [128, 128], BF16)) for i in range(2)]
        ps_k = [s2.enter_context(nc.psum_tensor(f"s1_pk{i}", [128, 512], F32)) for i in range(2)]
        ps_q = [s2.enter_context(nc.psum_tensor(f"s1_pq{i}", [128, 512], F32)) for i in range(2)]
        ps_v = [s2.enter_context(nc.psum_tensor(f"s1_pv{i}", [128, 512], F32)) for i in range(2)]

        bXp = (p.cnt["sX"], p.cnt["sX2"]); bV = p.cnt["sV"]; bS = p.cnt["sS"]
        bTt = p.cnt["sTt"]; bVt = p.cnt["sVt"]; bTm = p.cnt["sTm"]
        bE = p.cnt["sV2"]; bKQ = p.cnt["sKQ"]
        XSEM = lambda b: ("sX", "sX2")[b % 2]
        X_ = lambda b: bXp[b % 2] + 16 * (b // 2 + 1)
        V_ = lambda b, i: bV + 2 * b + i
        S_ = lambda b, i: bS + 2 * b + i
        Tt = lambda g: bTt + g + 1
        Vt = lambda g: bVt + g + 1
        Tm = lambda b, i: bTm + 3 * b + i
        E_ = lambda b, i: bE + 3 * b + i
        KQ = lambda b: bKQ + 16 * (b + 1)
        p.bump("sX", 16 * (NBLK // 2)); p.bump("sX2", 16 * (NBLK // 2))
        p.bump("sV", 2 * NBLK); p.bump("sS", 2 * NBLK)
        p.bump("sTt", 8 * NBLK); p.bump("sVt", 8 * NBLK); p.bump("sTm", 3 * NBLK)
        p.bump("sV2", 3 * NBLK); p.bump("sKQ", 16 * NBLK)
        wv_mark = p.bump("sC", 16 * 24)

        @block.gpsimd
        def _(g):
            for w_src, w_sb in ((wq_in, wqs), (wk_in, wks), (wv_in, wvs)):
                for c in range(NC):
                    g.dma_start(out=w_sb[:, c * E:(c + 1) * E],
                                in_=ap(w_src, c * 128 * E, [[E, 128], [1, E]])
                                ).then_inc(p.sem["sC"], 16)
            for b in range(NBLK):
                if b >= 2:
                    p.wait(g, "sS", S_(b - 2, 1))
                g.dma_start(out=xt[b % 2][:],
                            in_=ap(src, b * 128 * E, [[E, 128], [1, E]])).then_inc(p.sem[XSEM(b)], 16)

        @block.sync
        def _(sync):
            for b in range(NBLK):
                p.wait(sync, "sV2", E_(b, 1))
                sync.dma_start(out=ap(k1_hbm, b * 128, [[S, 128], [128 * S, NC], [1, 128]]),
                               in_=ktmp[b % 2][:]).then_inc(p.sem["sKQ"], 16)

        def kqv_evacs(v, b):
            bb = b % 2
            p.wait(v, "sTm", Tm(b, 1))
            if b >= 1:
                p.wait(v, "sKQ", KQ(b - 1))
            last = None
            for bank in range(2):
                for hp4 in range(4):
                    hp = bank * 4 + hp4
                    last = v.tensor_copy(ktmp[bb][:, hp * 128:(hp + 1) * 128],
                                         ps_k[bank][:, hp4 * 128:(hp4 + 1) * 128])
            last.then_inc(p.sem["sV2"], 1)
            p.wait(v, "sTm", Tm(b, 2))
            for bank in range(2):
                for hp4 in range(4):
                    hp = bank * 4 + hp4
                    src_ap = ap(ps_q[bank], hp4 * 128, [[512, 128], [16, 8], [1, 16]])
                    dst_ap = ap(q_sb, hp * S + 8 * b, [[NC * S, 128], [1, 8], [256, 16]])
                    last = v.tensor_copy(dst_ap, src_ap)
            last.then_inc(p.sem["sV2"], 1)
            p.wait(v, "sTm", Tm(b, 3))
            for half in range(2):
                src_ap = ap(ps_v[half], 0, [[512, 128], [64, 8], [1, 64]])
                dst_ap = ap(vaug, b * (H * 65) + half * 8 * 65,
                            [[NBLK * H * 65, 128], [65, 8], [1, 64]])
                last = v.tensor_copy(dst_ap, src_ap)
            last.then_inc(p.sem["sV2"], 1)

        @block.vector
        def _(v):
            v.memset(vaug[:], 1.0)
            for b in range(NBLK):
                bb = b % 2
                p.wait(v, XSEM(b), X_(b))
                if b >= 2:
                    p.wait(v, "sS", S_(b - 2, 1))
                ln_vec_pre(p, v, xt[bb][:], sml[:, bb:bb + 1], xcb[bb][:], "sV")
                p.wait(v, "sS", S_(b, 2))
                if b >= 2:
                    p.wait(v, "sTt", Tt((b - 2) * 8 + 7))
                v.tensor_scalar(xh[bb][:], xcb[bb][:], sml[:, 4 + bb:5 + bb], None,
                                ALU.mult).then_inc(p.sem["sV"], 1)
                for c in range(NC):
                    g = b * 8 + c
                    p.wait(v, "sTt", Tt(g))
                    v.tensor_copy(xT[bb][:, c * 128:(c + 1) * 128],
                                  ps_tr[c % 2][:]).then_inc(p.sem["sVt"], 1)
                if b >= 1:
                    kqv_evacs(v, b - 1)
            kqv_evacs(v, NBLK - 1)

        @block.scalar
        def _(sc):
            for b in range(NBLK):
                bb = b % 2
                p.wait(sc, "sV", V_(b, 1))
                if b >= 2:
                    p.wait(sc, "sV", V_(b - 2, 2))
                ln_scalar(p, sc, xcb[bb][:], xt[bb][:], sml[:, 2 + bb:3 + bb],
                          sml[:, 4 + bb:5 + bb], epsc[:, 0:1], "sS", S_(b, 1))

        @block.tensor
        def _(t):
            t.wait_ge(p.sem["sC"], wv_mark)
            t.wait_ge(p.sem["sW"], p.aux_constw)
            for b in range(NBLK):
                bb = b % 2
                p.wait(t, "sV", V_(b, 2))
                for c in range(NC):
                    g = b * 8 + c
                    if g >= 2:
                        p.wait(t, "sVt", Vt(g - 2))
                    t.transpose(ps_tr[c % 2][:], xh[bb][:, c * 128:(c + 1) * 128],
                                idb[:]).then_inc(p.sem["sTt"], 1)
                p.wait(t, "sVt", Vt(b * 8 + 7))
                if b >= 1:
                    p.wait(t, "sV2", E_(b - 1, 3))
                for wsb, psb in ((wks, ps_k), (wqs, ps_q)):
                    last = None
                    for bank in range(2):
                        for hp4 in range(4):
                            hp = bank * 4 + hp4
                            for c in range(NC):
                                last = t.matmul(psb[bank][:, hp4 * 128:(hp4 + 1) * 128],
                                                wsb[:, c * E + hp * 128: c * E + (hp + 1) * 128],
                                                xT[bb][:, c * 128:(c + 1) * 128],
                                                start=(hp4 == 0 and c == 0),
                                                stop=(hp4 == 3 and c == NC - 1))
                    last.then_inc(p.sem["sTm"], 1)
                last = None
                for half in range(2):
                    for c in range(NC):
                        last = t.matmul(ps_v[half][:],
                                        xT[bb][:, c * 128:(c + 1) * 128],
                                        wvs[:, c * E + half * 512: c * E + (half + 1) * 512],
                                        start=(c == 0), stop=(c == NC - 1))
                last.then_inc(p.sem["sTm"], 1)


    dstore = st.enter_context(nc.sbuf_tensor("s1_dstb", [128, 72 * 64], F32))
    cbc = st.enter_context(nc.sbuf_tensor("s1_cbc", [128, H * 64], F32))
    with contextlib.ExitStack() as s2, nc.Block() as block:
        ttile = [s2.enter_context(nc.sbuf_tensor(f"s1_tt{i}", [128, S], BF16)) for i in range(2)]
        ttc = s2.enter_context(nc.sbuf_tensor("s1_ttc", [128, 32 * 128], BF16))
        dtmp = s2.enter_context(nc.sbuf_tensor("s1_dtmp", [64, 128], BF16))
        crow = s2.enter_context(nc.sbuf_tensor("s1_crow", [128, 1], BF16))
        cvr = s2.enter_context(nc.sbuf_tensor("s1_cvr", [1, 64], BF16))
        ps_t1 = [s2.enter_context(nc.psum_tensor(f"s1_pt1{i}", [128, 128], BF16)) for i in range(2)]
        ps_d = s2.enter_context(nc.psum_tensor("s1_pd", [64, 128], F32))
        ps_d2 = s2.enter_context(nc.psum_tensor("s1_pd2", [128, 64], BF16))
        ps_c1 = s2.enter_context(nc.psum_tensor("s1_pc1", [1, 64], F32))
        ps_c2 = s2.enter_context(nc.psum_tensor("s1_pc2", [128, 64], F32))

        bWp = (p.cnt["sW"], p.cnt["sW2"]); bTt = p.cnt["sTt"]; bVt = p.cnt["sVt"]
        bTm = p.cnt["sTm"]; bE = p.cnt["sV2"]
        npair = [16 - g for g in range(8, 16)]
        WSEM = lambda it: ("sW", "sW2")[it % 2]
        W_ = lambda it: bWp[it % 2] + 16 * (it // 2 + 1)
        Tt = lambda q: bTt + q + 1
        Vt = lambda q: bVt + q + 1
        p.bump("sW", 16 * 8); p.bump("sW2", 16 * 8); p.bump("sTt", 32 * 16); p.bump("sVt", 32 * 16)
        crow_mark = p.bump("sKQ", 16)
        tm_marks = {}; v2_marks = {}
        tmc = bTm; v2c = bE
        for gi in range(8):
            for half in range(2):
                for pi in range(npair[gi]):
                    tm_marks[(gi, half, pi)] = (tmc + 1, tmc + 2); tmc += 2
                    v2_marks[(gi, half, pi)] = (v2c + 1, v2c + 2); v2c += 2
        cv_tm = {}; cv_v2 = {}
        for hh in range(H):
            cv_tm[hh] = (tmc + 1, tmc + 2); tmc += 2
            cv_v2[hh] = (v2c + 1, v2c + 2); v2c += 2
        p.bump("sTm", tmc - bTm); p.bump("sV2", v2c - bE)

        @block.sync
        def _(sync):
            sync.wait_ge(p.sem["sTG"], cast1)
            sync.dma_start(out=crow[:], in_=ap(tg_bf, 0, [[1, 128], [1, 1]])
                           ).then_inc(p.sem["sKQ"], 16)
            for gi in range(8):
                for half in range(2):
                    it = gi * 2 + half
                    if it >= 2:
                        p.wait(sync, "sTt", Tt((it - 1) * 32 - 1))
                    sync.dma_start(out=ttile[it % 2][:],
                                   in_=ap(tg_bf, (gi * 256 + half * 128) * S, [[S, 128], [1, S]])
                                   ).then_inc(p.sem[WSEM(it)], 16)

        @block.tensor
        def _(t):
            for gi in range(8):
                for half in range(2):
                    it = gi * 2 + half
                    p.wait(t, WSEM(it), W_(it))
                    for c in range(32):
                        q = it * 32 + c
                        if q >= 2:
                            p.wait(t, "sVt", Vt(q - 2))
                        t.transpose(ps_t1[c % 2][:], ttile[it % 2][:, c * 128:(c + 1) * 128],
                                    idb[:]).then_inc(p.sem["sTt"], 1)
                    p.wait(t, "sVt", Vt(it * 32 + 31))
                    for pi in range(npair[gi]):
                        hh = pi + 8 + gi
                        last = None
                        for c in range(32):
                            last = t.matmul(ps_d[:],
                                            vaug[:, c * (H * 65) + hh * 65: c * (H * 65) + hh * 65 + 64],
                                            ttc[:, c * 128:(c + 1) * 128],
                                            start=(c == 0), stop=(c == 31))
                        last.then_inc(p.sem["sTm"], 1)
                        p.wait(t, "sV2", v2_marks[(gi, half, pi)][0])
                        t.transpose(ps_d2[:], dtmp[:], idb[0:64, 0:64]).then_inc(p.sem["sTm"], 1)
            p.wait(t, "sKQ", crow_mark)
            for hh in range(H):
                last = None
                for c in range(32):
                    last = t.matmul(ps_c1[:], crow[:],
                                    vaug[:, c * (H * 65) + hh * 65: c * (H * 65) + hh * 65 + 64],
                                    start=(c == 0), stop=(c == 31))
                last.then_inc(p.sem["sTm"], 1)
                p.wait(t, "sV2", cv_v2[hh][0])
                t.matmul(ps_c2[:], onesb[:], cvr[:], start=True, stop=True).then_inc(p.sem["sTm"], 1)

        @block.vector
        def _(v):
            for gi in range(8):
                for half in range(2):
                    it = gi * 2 + half
                    for c in range(32):
                        q = it * 32 + c
                        p.wait(v, "sTt", Tt(q))
                        v.tensor_copy(ttc[:, c * 128:(c + 1) * 128],
                                      ps_t1[c % 2][:]).then_inc(p.sem["sVt"], 1)
                    for pi in range(npair[gi]):
                        p.wait(v, "sTm", tm_marks[(gi, half, pi)][0])
                        v.tensor_copy(dtmp[:], ps_d[:]).then_inc(p.sem["sV2"], 1)
                        p.wait(v, "sTm", tm_marks[(gi, half, pi)][1])
                        pid = PAIR_IDX[(pi + 8 + gi, pi)]
                        v.tensor_copy(dstore[:, (pid * 2 + half) * 64:(pid * 2 + half + 1) * 64],
                                      ps_d2[:]).then_inc(p.sem["sV2"], 1)
            for hh in range(H):
                p.wait(v, "sTm", cv_tm[hh][0])
                v.tensor_copy(cvr[:], ps_c1[:]).then_inc(p.sem["sV2"], 1)
                p.wait(v, "sTm", cv_tm[hh][1])
                v.tensor_copy(cbc[:, hh * 64:(hh + 1) * 64], ps_c2[:]).then_inc(p.sem["sV2"], 1)
        p.check("sV2", v2c)

    with contextlib.ExitStack() as s2, nc.Block() as block:
        kst = [s2.enter_context(nc.sbuf_tensor(f"s1_kst{i}", [128, S], BF16)) for i in range(2)]
        pt = [s2.enter_context(nc.sbuf_tensor(f"s1_pts{i}", [128, 1024], BF16)) for i in range(2)]
        outT = [s2.enter_context(nc.sbuf_tensor(f"s1_oT{i}", [128, 512], F32)) for i in range(2)]
        att = [s2.enter_context(nc.sbuf_tensor(f"s1_att{i}", [128, E], F32)) for i in range(4)]
        xr = s2.enter_context(nc.sbuf_tensor("s1_xr", [128, E], F32))
        gam = s2.enter_context(nc.sbuf_tensor("s1_gam", [128, E], BF16))
        bet = s2.enter_context(nc.sbuf_tensor("s1_bet", [128, E], BF16))
        sdump = s2.enter_context(nc.sbuf_tensor("s1_sdmp", [128, E], F32))
        sml = s2.enter_context(nc.sbuf_tensor("s1_sm2", [128, 12], F32))
        rc = s2.enter_context(nc.sbuf_tensor("s1_rc", [128, 2], F32))
        ps_s = [s2.enter_context(nc.psum_tensor(f"s1_pss{i}", [128, 1024], F32)) for i in range(2)]
        ps_o = [s2.enter_context(nc.psum_tensor(f"s1_po{i}", [65, 512], F32)) for i in range(2)]
        ps_a = [s2.enter_context(nc.psum_tensor(f"s1_pa{i}", [128, 128], F32)) for i in range(2)]
        zb = s2.enter_context(nc.sbuf_tensor("s1_zb", [128, 128], BF16))

        bKQp = (p.cnt["sKQ"], p.cnt["sKQ2"]); bCRp = (p.cnt["sCR"], p.cnt["sCR2"])
        bST = p.cnt["sST"]; bP = p.cnt["sP"]
        bPV = p.cnt["sPV"]; bOE = p.cnt["sOE"]; bTt = p.cnt["sTt"]; bE = p.cnt["sV2"]
        bX = p.cnt["sX"]; bV = p.cnt["sV"]
        gb_mark = p.bump("sGD", 32)
        bD = p.cnt["sGD"]
        bLx = p.cnt["sLx"]; bLr = p.cnt["sLr"]
        MP = 64
        KQAS = lambda mp: ("sKQ", "sKQ2")[mp % 2]
        KQAV = lambda mp: bKQp[mp % 2] + 16 * (mp // 2 + 1)
        KQBS = lambda mp: ("sCR", "sCR2")[mp % 2]
        KQBV = lambda mp: bCRp[mp % 2] + 16 * (mp // 2 + 1)
        STm = lambda mp, j, h: bST + 32 * mp + 2 * j + 1 + h
        EXm = lambda mp, j, h: bP + 32 * mp + 2 * j + 1 + h
        PVm = lambda mp, j, h: bPV + 32 * mp + 2 * j + 1 + h
        OEm = lambda mp, h: bOE + 2 * mp + 1 + h
        Tt = lambda u: bTt + u + 1
        E_ = lambda u: bE + u + 1
        XR = lambda i: bX + 16 * (i + 1)
        D_ = lambda i: bD + 16 * (i + 1)
        XN = lambda i: bV + i + 1
        Lx = lambda i: bLx + i + 1
        Lr = lambda i: bLr + 2 * (i + 1)
        p.bump("sKQ", 16 * (MP // 2)); p.bump("sKQ2", 16 * (MP // 2))
        p.bump("sCR", 16 * (MP // 2)); p.bump("sCR2", 16 * (MP // 2))
        p.bump("sST", 32 * MP); p.bump("sP", 32 * MP); p.bump("sPV", 32 * MP)
        p.bump("sOE", 2 * MP); p.bump("sTt", 8 * MP); p.bump("sV2", 8 * MP)
        p.bump("sX", 16 * 32); p.bump("sGD", 16 * 32); p.bump("sV", 32)
        p.bump("sLx", 32); p.bump("sLr", 64)

        @block.sync
        def _(sync):
            for mp in range(MP):
                qg, hhp = divmod(mp, 8)
                hp = hhp
                if mp >= 2:
                    p.wait(sync, "sST", STm(mp - 2, 15, 1))
                sync.dma_start(out=kst[mp % 2][0:64, :],
                               in_=ap(k1_hbm, hp * 128 * S, [[S, 64], [1, S]])
                               ).then_inc(p.sem[KQAS(mp)], 16)
                sync.dma_start(out=kst[mp % 2][64:128, :],
                               in_=ap(k1_hbm, hp * 128 * S + 64 * S, [[S, 64], [1, S]])
                               ).then_inc(p.sem[KQBS(mp)], 16)

        @block.gpsimd
        def _(g):
            g.dma_start(out=gam[:], in_=gamma_bc_in).then_inc(p.sem["sGD"], 16)
            g.dma_start(out=bet[:], in_=beta_bc_in).then_inc(p.sem["sGD"], 16)

            def store(i):
                qg, c = divmod(i, 4)
                qm = 2 * qg + c // 2; qdh = c % 2
                p.wait(g, "sV", XN(i))
                g.dma_start(out=ap(dst, (2048 * qdh + qm) * E, [[16 * E, 128], [1, E]]),
                            in_=xr[:]).then_inc(p.sem["sGD"], 16)
            for i in range(32):
                qg, c = divmod(i, 4)
                qm = 2 * qg + c // 2; qdh = c % 2
                if i >= 1:
                    store(i - 1)
                    p.wait(g, "sGD", D_(i - 1))
                g.dma_start(out=xr[:],
                            in_=ap(src, (2048 * qdh + qm) * E, [[16 * E, 128], [1, E]])
                            ).then_inc(p.sem["sX"], 16)
            store(31)

        @block.tensor
        def _(t):
            t.wait_ge(p.sem["sZB"], p.bump("sZB", 1))
            fk = [0]

            def fill(n):
                for _ in range(n):
                    o = (fk[0] % 7) * 512
                    t.matmul(ps_o[fk[0] % 2][:], zb[:, 0:65], q_sb[:, o:o + 512],
                             start=False, stop=False, skip_group_check=True)
                    fk[0] += 1

            for mp in range(MP):
                qg, hhp = divmod(mp, 8)
                hp = hhp

                def s_grp(j, h):
                    hq = h * 64
                    if mp > 0 or j >= 1:
                        pj, pm = (j - 1, mp) if j >= 1 else (15, mp - 1)
                        p.wait(t, "sP", EXm(pm, pj, h))
                    for h2 in range(2):
                        kt = 2 * j + h2
                        ins = t.matmul(ps_s[h][:, h2 * 512:(h2 + 1) * 512],
                                       kst[mp % 2][hq:hq + 64, kt * 128:(kt + 1) * 128],
                                       q_sb[hq:hq + 64, hp * S + qg * 512: hp * S + (qg + 1) * 512],
                                       start=True, stop=True)
                        if h2 == 1:
                            ins.then_inc(p.sem["sST"], 1)

                def pv_grp(j, h):
                    hh = 2 * hhp + h
                    p.wait(t, "sP", EXm(mp, j, h))
                    if j == 0 and mp >= 1:
                        p.wait(t, "sOE", OEm(mp - 1, h))
                    for h2 in range(2):
                        kt = 2 * j + h2
                        ins = t.matmul(ps_o[h][:],
                                       vaug[:, kt * (H * 65) + hh * 65: kt * (H * 65) + (hh + 1) * 65],
                                       pt[h][:, h2 * 512:(h2 + 1) * 512],
                                       start=(kt == 0), stop=(kt == 31))
                        if h2 == 1:
                            ins.then_inc(p.sem["sPV"], 1)

                p.wait(t, KQAS(mp), KQAV(mp))
                p.wait(t, KQBS(mp), KQBV(mp))
                s_grp(0, 0)
                s_grp(0, 1)
                if mp >= 1:
                    for c in range(8):
                        u = 8 * (mp - 1) + c
                        p.wait(t, "sOE", OEm(mp - 1, c // 4))
                        if u >= 2:
                            p.wait(t, "sV2", E_(u - 2))
                        t.transpose(ps_a[u % 2][:],
                                    outT[c // 4][:, (c % 4) * 128:(c % 4 + 1) * 128],
                                    idf[:]).then_inc(p.sem["sTt"], 1)
                for j in range(1, 16):
                    if j >= 2:
                        fill(3)
                    s_grp(j, 0)
                    s_grp(j, 1)
                    pv_grp(j - 1, 0)
                    pv_grp(j - 1, 1)
                pv_grp(15, 0)
                pv_grp(15, 1)
            for c in range(8):
                u = 8 * (MP - 1) + c
                p.wait(t, "sOE", OEm(MP - 1, c // 4))
                p.wait(t, "sV2", E_(u - 2))
                t.transpose(ps_a[u % 2][:],
                            outT[c // 4][:, (c % 4) * 128:(c % 4 + 1) * 128],
                            idf[:]).then_inc(p.sem["sTt"], 1)

        @block.scalar
        def _(sc):
            for mp in range(MP):
                qg, hhp = divmod(mp, 8)
                for j in range(16):
                    for h in range(2):
                        p.wait(sc, "sST", STm(mp, j, h))
                        if mp > 0 or j >= 1:
                            pj, pm = (j - 1, mp) if j >= 1 else (15, mp - 1)
                            p.wait(sc, "sPV", PVm(pm, pj, h))
                        sc.activation(pt[h][:], ps_s[h][:], AF.Exp,
                                      scale=1.0 / 32.0).then_inc(p.sem["sP"], 1)
                if hhp == 7:
                    for c in range(4):
                        i = 4 * qg + c
                        p.wait(sc, "sLx", Lx(i))
                        ln_scalar(p, sc, att[c][:], sdump[:], sml[:, 4 + c:5 + c],
                                  sml[:, 8 + c:9 + c], epsc[:, 1:2], "sLr", Lr(i) - 1)

        @block.vector
        def _(v):
            v.memset(zb[:], 0.0).then_inc(p.sem["sZB"], 1)
            cc = p.inc(v.memset(outT[0][:], 0.0), "sVc")
            p.wait(v, "sVc", cc)
            cc = p.inc(v.memset(outT[1][:], 0.0), "sVc")
            p.wait(v, "sVc", cc)
            for mp in range(MP):
                qg, hhp = divmod(mp, 8)
                for h in range(2):
                    p.wait(v, "sPV", PVm(mp, 15, h))
                    if mp >= 1:
                        p.wait(v, "sTt", Tt(8 * (mp - 1) + 4 * h + 3))
                    v.tensor_copy(outT[h][0:65, :], ps_o[h][:]).then_inc(p.sem["sOE"], 1)
                for c in range(8):
                    u = 8 * mp + c
                    h = c // 4
                    hh = 2 * hhp + h
                    cc4 = c % 4
                    qm = 2 * qg + cc4 // 2; qdh = cc4 % 2
                    p.wait(v, "sTt", Tt(u))
                    if mp >= 8 and hhp == 0:
                        p.wait(v, "sLr", Lr(4 * (qg - 1) + cc4))
                    cc = p.inc(v.reciprocal(rc[:, u % 2:u % 2 + 1], ps_a[u % 2][:, 64:65]), "sVc")
                    p.wait(v, "sVc", cc)
                    if (hh, qm) in PAIR_IDX:
                        pid = PAIR_IDX[(hh, qm)]
                        bias_ap = dstore[:, (pid * 2 + qdh) * 64:(pid * 2 + qdh + 1) * 64]
                    else:
                        bias_ap = cbc[:, hh * 64:(hh + 1) * 64]
                    v.scalar_tensor_tensor(att[cc4][:, hh * 64:(hh + 1) * 64],
                                           ps_a[u % 2][:, 0:64], rc[:, u % 2:u % 2 + 1],
                                           bias_ap, ALU.mult, ALU.add).then_inc(p.sem["sV2"], 1)
                if hhp == 7:
                    if qg == 0:
                        p.wait(v, "sGD", gb_mark)
                    for c in range(4):
                        i = 4 * qg + c
                        ln_vec_pre(p, v, att[c][:], sml[:, c:c + 1], att[c][:], "sLx")
                    for c in range(4):
                        i = 4 * qg + c
                        p.wait(v, "sLr", Lr(i))
                        cc = p.inc(v.tensor_scalar(att[c][:], att[c][:], sml[:, 8 + c:9 + c], None,
                                                   ALU.mult), "sVc")
                        p.wait(v, "sVc", cc)
                        cc = p.inc(v.tensor_mul(att[c][:], att[c][:], gam[:]), "sVc")
                        p.wait(v, "sVc", cc)
                        cc = p.inc(v.tensor_add(att[c][:], att[c][:], bet[:]), "sVc")
                        p.wait(v, "sVc", cc)
                        p.wait(v, "sX", XR(i))
                        v.tensor_add(xr[:], att[c][:], xr[:]).then_inc(p.sem["sV"], 1)


def emit_moe(nc, p, src, dst, wgu_bf, wd_bf, wr_in, ct_dram, idb, idf, onesb, epsc, cast_all, tag, gate):
    with contextlib.ExitStack() as st:
        xTf = st.enter_context(nc.sbuf_tensor(f"{tag}_xT", [128, NC * S], BF16))

        with contextlib.ExitStack() as s2, nc.Block() as block:
            xt = [s2.enter_context(nc.sbuf_tensor(f"{tag}_xt{i}", [128, E], F32)) for i in range(2)]
            xcb = [s2.enter_context(nc.sbuf_tensor(f"{tag}_xc{i}", [128, E], F32)) for i in range(2)]
            xh = [s2.enter_context(nc.sbuf_tensor(f"{tag}_xh{i}", [128, E], BF16)) for i in range(2)]
            sqd = s2.enter_context(nc.sbuf_tensor(f"{tag}_sqd", [128, E], F32))
            sml = s2.enter_context(nc.sbuf_tensor(f"{tag}_sm", [128, 6], F32))
            wrf = s2.enter_context(nc.sbuf_tensor(f"{tag}_wrf", [128, NC * 8], F32))
            wrs = s2.enter_context(nc.sbuf_tensor(f"{tag}_wrs", [128, NC * 8], BF16))
            lg = s2.enter_context(nc.sbuf_tensor(f"{tag}_lg", [128, NBLK * 8], F32))
            lg2 = s2.enter_context(nc.sbuf_tensor(f"{tag}_lg2", [128, NBLK * 8], F32))
            eqt = s2.enter_context(nc.sbuf_tensor(f"{tag}_eqt", [128, NBLK * 8], F32))
            tps = s2.enter_context(nc.sbuf_tensor(f"{tag}_tps", [128, NBLK * 4], F32))
            ps_tr = [s2.enter_context(nc.psum_tensor(f"{tag}_ptr{i}", [128, 128], BF16)) for i in range(2)]
            ps_lg = s2.enter_context(nc.psum_tensor(f"{tag}_plg", [128, 8], F32))
            ps_ct = [s2.enter_context(nc.psum_tensor(f"{tag}_pct{i}", [8, 128], F32)) for i in range(2)]
            cttmp = [s2.enter_context(nc.sbuf_tensor(f"{tag}_ctt{i}", [8, 128], BF16)) for i in range(2)]

            bXp = (p.cnt["sX"], p.cnt["sX2"]); bV = p.cnt["sV"]; bS = p.cnt["sS"]
            bTt = p.cnt["sTt"]; bVt = p.cnt["sVt"]; bTm = p.cnt["sTm"]; bE = p.cnt["sV2"]
            bOE = p.cnt["sOE"]; bLr = p.cnt["sLr"]
            XSEM = lambda b: ("sX", "sX2")[b % 2]
            X_ = lambda b: bXp[b % 2] + 16 * (b // 2 + 1)
            V_ = lambda b, i: bV + 2 * b + i
            S_ = lambda b, i: bS + 2 * b + i
            Tt = lambda g: bTt + g + 1
            Vt = lambda g: bVt + g + 1
            Tm = lambda b: bTm + b + 1
            E_ = lambda b: bE + b + 1
            p.bump("sX", 16 * (NBLK // 2)); p.bump("sX2", 16 * (NBLK // 2))
            p.bump("sV", 2 * NBLK); p.bump("sS", 2 * NBLK)
            p.bump("sTt", 8 * NBLK); p.bump("sVt", 8 * NBLK)
            p.bump("sTm", NBLK); p.bump("sV2", NBLK)
            wr_mark = p.bump("sW", 16)
            ST5 = bOE + 1; ST6 = bOE + 2; ST9 = bOE + 3
            p.bump("sOE", 3)
            EXD = bLr + 1
            p.bump("sLr", 1)
            bTm2 = p.cnt["sTm"]; bE2 = p.cnt["sV2"]
            TmC = lambda b: bTm2 + b + 1
            E_C = lambda b: bE2 + b + 1
            p.bump("sTm", NBLK); p.bump("sV2", NBLK)
            bDDp = (p.cnt["sD"], p.cnt["sD2"])
            CTSEM = lambda b: ("sD", "sD2")[b % 2]
            CTS = lambda b: bDDp[b % 2] + 16 * (b // 2 + 1)
            p.bump("sD", 16 * (NBLK // 2)); p.bump("sD2", 16 * (NBLK // 2))

            @block.sync
            def _(sync):
                sync.wait_ge(p.sem["sTG"], cast_all[0])
                sync.wait_ge(p.sem["sC2"], cast_all[1])
                sync.dma_start(out=wrf[:], in_=ap(wr_in, 0, [[8, 128], [128 * 8, NC], [1, 8]])
                               ).then_inc(p.sem["sW"], 16)
                for b in range(NBLK):
                    p.wait(sync, "sV2", E_C(b))
                    sync.dma_start(out=ap(ct_dram, b * 128, [[S, 8], [1, 128]]),
                                   in_=cttmp[b % 2][:]).then_inc(p.sem[CTSEM(b)], 16)

            @block.gpsimd
            def _(g):
                p.wait(g, "sD", gate[0])
                p.wait(g, "sD2", gate[1])
                p.wait(g, "sGD", gate[2])
                for b in range(NBLK):
                    if b >= 2:
                        p.wait(g, "sS", S_(b - 2, 1))
                    g.dma_start(out=xt[b % 2][:],
                                in_=ap(src, b * 128 * E, [[E, 128], [1, E]])).then_inc(p.sem[XSEM(b)], 16)

            @block.vector
            def _(v):
                p.wait(v, "sW", wr_mark)
                v.tensor_copy(wrs[:], wrf[:])
                for b in range(NBLK):
                    bb = b % 2
                    p.wait(v, XSEM(b), X_(b))
                    if b >= 2:
                        p.wait(v, "sS", S_(b - 2, 1))
                    ln_vec_pre(p, v, xt[bb][:], sml[:, bb:bb + 1], xcb[bb][:], "sV")
                    p.wait(v, "sS", S_(b, 2))
                    if b >= 2:
                        p.wait(v, "sTt", Tt((b - 2) * 8 + 7))
                    v.tensor_scalar(xh[bb][:], xcb[bb][:], sml[:, 4 + bb:5 + bb], None,
                                    ALU.mult).then_inc(p.sem["sV"], 1)
                    for c in range(NC):
                        g = b * 8 + c
                        p.wait(v, "sTt", Tt(g))
                        v.tensor_copy(xTf[:, c * S + b * 128:c * S + (b + 1) * 128],
                                      ps_tr[c % 2][:]).then_inc(p.sem["sVt"], 1)
                    if b >= 1:
                        p.wait(v, "sTm", Tm(b - 1))
                        v.tensor_copy(lg[:, 8 * (b - 1):8 * b], ps_lg[:]).then_inc(p.sem["sV2"], 1)
                p.wait(v, "sTm", Tm(NBLK - 1))
                v.tensor_copy(lg[:, 8 * (NBLK - 1):8 * NBLK], ps_lg[:]).then_inc(p.sem["sV2"], 1)

            @block.scalar
            def _(sc):
                for b in range(NBLK):
                    bb = b % 2
                    p.wait(sc, "sV", V_(b, 1))
                    if b >= 2:
                        p.wait(sc, "sV", V_(b - 2, 2))
                    ln_scalar(p, sc, xcb[bb][:], sqd[:], sml[:, 2 + bb:3 + bb],
                              sml[:, 4 + bb:5 + bb], epsc[:, 0:1], "sS", S_(b, 1))
                p.wait(sc, "sOE", ST5)
                sc.activation(tps[:, 2 * NBLK:3 * NBLK], tps[:, 2 * NBLK:3 * NBLK],
                              AF.Exp).then_inc(p.sem["sLr"], 1)

            @block.tensor
            def _(t):
                for b in range(NBLK):
                    bb = b % 2
                    p.wait(t, "sV", V_(b, 2))
                    for c in range(NC):
                        g = b * 8 + c
                        if g >= 2:
                            p.wait(t, "sVt", Vt(g - 2))
                        t.transpose(ps_tr[c % 2][:], xh[bb][:, c * 128:(c + 1) * 128],
                                    idb[:]).then_inc(p.sem["sTt"], 1)
                    p.wait(t, "sVt", Vt(b * 8 + 7))
                    if b >= 1:
                        p.wait(t, "sV2", E_(b - 1))
                    last = None
                    for c in range(NC):
                        last = t.matmul(ps_lg[:], xTf[:, c * S + b * 128:c * S + (b + 1) * 128],
                                        wrs[:, c * 8:(c + 1) * 8],
                                        start=(c == 0), stop=(c == NC - 1))
                    last.then_inc(p.sem["sTm"], 1)
                p.wait(t, "sOE", ST9)
                for b in range(NBLK):
                    if b >= 2:
                        p.wait(t, "sV2", E_C(b - 2))
                    t.transpose(ps_ct[b % 2][:], eqt[:, 8 * b:8 * (b + 1)], idf[:]
                                ).then_inc(p.sem["sTm"], 1)

            @block.vector
            def _(v):
                N2 = NBLK
                cc = p.inc(v.memset(tps[:, 0:1], 0.0), "sVc")
                p.wait(v, "sV2", E_(N2 - 1))
                p.wait(v, "sVc", cc)

                def stage(opfn):
                    for b in range(N2):
                        ins = opfn(b)
                    c2 = p.inc(ins, "sVc")
                    p.wait(v, "sVc", c2)

                stage(lambda b: v.reduce_max(tps[:, b:b + 1], lg[:, 8 * b:8 * (b + 1)], axis=AX))
                stage(lambda b: v.tensor_scalar(eqt[:, 8 * b:8 * (b + 1)], lg[:, 8 * b:8 * (b + 1)],
                                                tps[:, b:b + 1], None, ALU.is_equal))
                stage(lambda b: v.scalar_tensor_tensor(lg2[:, 8 * b:8 * (b + 1)], eqt[:, 8 * b:8 * (b + 1)],
                                                       NEG, lg[:, 8 * b:8 * (b + 1)], ALU.mult, ALU.add))
                stage(lambda b: v.reduce_max(tps[:, N2 + b:N2 + b + 1], lg2[:, 8 * b:8 * (b + 1)], axis=AX))
                for b in range(N2):
                    ins = v.tensor_sub(tps[:, 2 * N2 + b:2 * N2 + b + 1],
                                       tps[:, N2 + b:N2 + b + 1], tps[:, b:b + 1])
                ins.then_inc(p.sem["sOE"], 1)
                p.wait(v, "sLr", EXD)
                c = p.inc(v.tensor_scalar(tps[:, 3 * N2:4 * N2], tps[:, 2 * N2:3 * N2], 1.0, None,
                                          ALU.add), "sVc")
                p.wait(v, "sVc", c)
                c = p.inc(v.reciprocal(tps[:, 3 * N2:4 * N2], tps[:, 3 * N2:4 * N2]), "sVc")
                p.wait(v, "sVc", c)
                c = p.inc(v.tensor_scalar_mul(tps[:, 3 * N2:4 * N2], tps[:, 3 * N2:4 * N2], DS), "sVc")
                p.wait(v, "sVc", c)
                c = p.inc(v.tensor_mul(tps[:, 2 * N2:3 * N2], tps[:, 2 * N2:3 * N2],
                                       tps[:, 3 * N2:4 * N2]), "sVc")
                p.wait(v, "sVc", c)
                v.memset(tps[:, 0:1], 0.0).then_inc(p.sem["sOE"], 1)
                stage(lambda b: v.tensor_scalar(eqt[:, 8 * b:8 * (b + 1)], eqt[:, 8 * b:8 * (b + 1)],
                                                tps[:, 3 * N2 + b:3 * N2 + b + 1], None, ALU.mult))
                stage(lambda b: v.tensor_scalar(lg2[:, 8 * b:8 * (b + 1)], lg[:, 8 * b:8 * (b + 1)],
                                                tps[:, N2 + b:N2 + b + 1], None, ALU.is_equal))
                for b in range(N2):
                    ins = v.scalar_tensor_tensor(eqt[:, 8 * b:8 * (b + 1)], lg2[:, 8 * b:8 * (b + 1)],
                                                 tps[:, 2 * N2 + b:2 * N2 + b + 1],
                                                 eqt[:, 8 * b:8 * (b + 1)], ALU.mult, ALU.add)
                ins.then_inc(p.sem["sOE"], 1)
                for b in range(N2):
                    p.wait(v, "sTm", TmC(b))
                    if b >= 2:
                        p.wait(v, CTSEM(b - 2), CTS(b - 2))
                    v.tensor_copy(cttmp[b % 2][:], ps_ct[b % 2][:]).then_inc(p.sem["sV2"], 1)
            a_final_vt = p.cnt["sVt"]
            a_final_v2 = p.cnt["sV2"]
            a_cts = (p.cnt["sD"], p.cnt["sD2"])

        with contextlib.ExitStack() as s2, nc.Block() as block:
            h_sb = s2.enter_context(nc.sbuf_tensor(f"{tag}_h", [128, 32 * 512], BF16))
            oacc = s2.enter_context(nc.sbuf_tensor(f"{tag}_oacc", [128, NC * 512], F32))
            otmp = s2.enter_context(nc.sbuf_tensor(f"{tag}_otmp", [128, 4 * 512], F32))
            cbce = s2.enter_context(nc.sbuf_tensor(f"{tag}_cbc", [128, NE * 512], F32))
            wgs = [s2.enter_context(nc.sbuf_tensor(f"{tag}_wg{i}", [128, 4 * 1024], BF16)) for i in range(2)]
            wus = [s2.enter_context(nc.sbuf_tensor(f"{tag}_wu{i}", [128, 4 * 1024], BF16)) for i in range(2)]
            wds = [s2.enter_context(nc.sbuf_tensor(f"{tag}_wd{i}", [128, 4 * 512], BF16)) for i in range(2)]
            sg = [s2.enter_context(nc.sbuf_tensor(f"{tag}_sg{i}", [128, 512], BF16)) for i in range(2)]
            crw = [s2.enter_context(nc.sbuf_tensor(f"{tag}_crw{i}", [1, NE * 512], BF16)) for i in range(2)]
            xr = s2.enter_context(nc.sbuf_tensor(f"{tag}_xr", [128, E], F32))
            xnw = s2.enter_context(nc.sbuf_tensor(f"{tag}_xnw", [128, E], F32))
            ps_g = s2.enter_context(nc.psum_tensor(f"{tag}_pg", [128, 512], F32))
            ps_u = s2.enter_context(nc.psum_tensor(f"{tag}_pu", [128, 512], F32))
            ps_o = [s2.enter_context(nc.psum_tensor(f"{tag}_po{i}", [128, 512], F32)) for i in range(4)]
            ps_x = [s2.enter_context(nc.psum_tensor(f"{tag}_px{i}", [128, 512], F32)) for i in range(2)]

            bWp = [p.cnt["sW"], p.cnt["sW2"]]; bKQp = [p.cnt["sKQ"], p.cnt["sKQ2"]]
            bCRp = [p.cnt["sCR"], p.cnt["sCR2"]]
            bTm = p.cnt["sTm"]; bP = p.cnt["sP"]; bVt = p.cnt["sVt"]; bPV = p.cnt["sPV"]
            bE = p.cnt["sV2"]; bTt = p.cnt["sTt"]; bX = p.cnt["sX"]; bD = p.cnt["sGD"]
            mWG = {}; mWD = {}; mGU = {}; mSI = {}; mH = {}; mDG = {}; mOE = {}; mCB_t = {}; mCB_v = {}
            mTR = {}; mAD = {}; mXR = {}; mST = {}; mXN = {}
            cWp = list(bWp); cKQp = list(bKQp); cCRp = list(bCRp)
            cTm = bTm; cP = bP; cVt = bVt; cPV = bPV
            cE = bE; cTt = bTt; cX = bX; cD = bD
            jWG = 0; jWD = 0
            mCR = {}
            bOEm = p.cnt["sOE"]
            OAC = lambda tt: bOEm + tt + 1
            p.bump("sOE", 8)
            WGSEM = lambda j: ("sW", "sW2")[j % 2]
            WDSEM = lambda j: ("sKQ", "sKQ2")[j % 2]
            CRSEM = lambda tt: ("sCR", "sCR2")[tt % 2]
            for tt in range(8):
                cCRp[tt % 2] += 16
                mCR[tt] = cCRp[tt % 2]
                for e in range(NE):
                    mCB_t[(tt, e)] = cTm = cTm + 1
                    mCB_v[(tt, e)] = cE = cE + 1
                for e in range(NE):
                    for grp in range(8):
                        cWp[jWG % 2] += 32
                        mWG[(tt, e, grp)] = (jWG, cWp[jWG % 2])
                        jWG += 1
                    for ht in range(32):
                        mGU[(tt, e, ht, 0)] = cTm = cTm + 1
                        mGU[(tt, e, ht, 1)] = cTm = cTm + 1
                        mSI[(tt, e, ht)] = cP = cP + 1
                        mH[(tt, e, ht)] = cVt = cVt + 1
                    for half in range(2):
                        for g4 in range(8):
                            cKQp[jWD % 2] += 16
                            mWD[(tt, e, half, g4)] = (jWD, cKQp[jWD % 2])
                            jWD += 1
                            mDG[(tt, e, half, g4)] = cPV = cPV + 1
                        mOE[(tt, e, half)] = cE = cE + 1
                for r in range(4):
                    mXR[(tt, r)] = cX = cX + 16
                    for c in range(8):
                        mTR[(tt, r, c)] = cTt = cTt + 1
                        mAD[(tt, r, c)] = cE = cE + 1
                    mXN[(tt, r)] = mAD[(tt, r, 7)]
                    mST[(tt, r)] = cD = cD + 16
            p.bump("sW", cWp[0] - bWp[0]); p.bump("sW2", cWp[1] - bWp[1])
            p.bump("sKQ", cKQp[0] - bKQp[0]); p.bump("sKQ2", cKQp[1] - bKQp[1])
            p.bump("sCR", cCRp[0] - bCRp[0]); p.bump("sCR2", cCRp[1] - bCRp[1])
            p.bump("sTm", cTm - bTm)
            p.bump("sP", cP - bP); p.bump("sVt", cVt - bVt); p.bump("sPV", cPV - bPV)
            p.bump("sV2", cE - bE); p.bump("sTt", cTt - bTt); p.bump("sX", cX - bX)
            p.bump("sGD", cD - bD)
            WG_ORD = sorted(mWG.items(), key=lambda kv: kv[1][0])
            GU_LAST = {}
            for (tt, e, grp), (j, _) in mWG.items():
                GU_LAST[j] = mGU[(tt, e, grp * 4 + 3, 1)]
            DG_BYJ = {}
            for (tt, e, half, g4), (j, _) in mWD.items():
                DG_BYJ[j] = mDG[(tt, e, half, g4)]

            @block.sync
            def _(sync):
                sync.wait_ge(p.sem["sD"], a_cts[0])
                sync.wait_ge(p.sem["sD2"], a_cts[1])
                done_cr = set()
                for (tt, e, grp), (j, wv) in WG_ORD:
                    if tt not in done_cr:
                        done_cr.add(tt)
                        if tt >= 2:
                            p.wait(sync, "sTm", mCB_t[(tt - 2, NE - 1)])
                        sync.dma_start(out=crw[tt % 2][:],
                                       in_=ap(ct_dram, tt * 512, [[1, 1], [S, NE], [1, 512]])
                                       ).then_inc(p.sem[CRSEM(tt)], 16)
                    if j >= 2:
                        p.wait(sync, "sTm", GU_LAST[j - 2])
                    base_g = ((0 * NE + e) * 32 + grp * 4) * 128 * 1024
                    base_u = ((1 * NE + e) * 32 + grp * 4) * 128 * 1024
                    sync.dma_start(out=wgs[j % 2][:],
                                   in_=ap(wgu_bf, base_g, [[1024, 128], [128 * 1024, 4], [1, 1024]])
                                   ).then_inc(p.sem[WGSEM(j)], 16)
                    sync.dma_start(out=wus[j % 2][:],
                                   in_=ap(wgu_bf, base_u, [[1024, 128], [128 * 1024, 4], [1, 1024]])
                                   ).then_inc(p.sem[WGSEM(j)], 16)
                    if grp == 7:
                        for half in range(2):
                            for g4 in range(8):
                                j2, wv2 = mWD[(tt, e, half, g4)]
                                if j2 >= 2:
                                    p.wait(sync, "sPV", DG_BYJ[j2 - 2])
                                sync.dma_start(out=wds[j2 % 2][:],
                                               in_=ap(wd_bf, e * HID * E + g4 * 512 * E + half * 512,
                                                      [[E, 128], [128 * E, 4], [1, 512]])
                                               ).then_inc(p.sem[WDSEM(j2)], 16)

            @block.scalar
            def _(sc):
                for tt in range(8):
                    for e in range(NE):
                        for ht in range(32):
                            p.wait(sc, "sTm", mGU[(tt, e, ht, 0)])
                            if ht >= 2:
                                p.wait(sc, "sVt", mH[(tt, e, ht - 2)])
                            sc.activation(sg[ht % 2][:], ps_g[:], AF.Silu
                                          ).then_inc(p.sem["sP"], 1)

            @block.gpsimd
            def _(g):
                p.wait(g, "sD", gate[0])
                p.wait(g, "sD2", gate[1])
                p.wait(g, "sGD", gate[2])
                for tt in range(8):
                    for r in range(4):
                        i = tt * 4 + r
                        if i >= 1:
                            pr = divmod(i - 1, 4)
                            p.wait(g, "sV2", mXN[(pr[0], pr[1])])
                            g.dma_start(
                                out=ap(dst, (pr[0] * 512 + pr[1] * 128) * E, [[E, 128], [1, E]]),
                                in_=xnw[:]).then_inc(p.sem["sGD"], 16)
                        g.dma_start(out=xr[:],
                                    in_=ap(src, (tt * 512 + r * 128) * E, [[E, 128], [1, E]])
                                    ).then_inc(p.sem["sX"], 16)
                p.wait(g, "sV2", mXN[(7, 3)])
                g.dma_start(out=ap(dst, (7 * 512 + 3 * 128) * E, [[E, 128], [1, E]]),
                            in_=xnw[:]).then_inc(p.sem["sGD"], 16)

            @block.tensor
            def _(t):
                t.wait_ge(p.sem["sVt"], a_final_vt)
                t.wait_ge(p.sem["sV2"], a_final_v2)
                for tt in range(8):
                    p.wait(t, CRSEM(tt), mCR[tt])
                    for e in range(NE):
                        if tt > 0:
                            p.wait(t, "sV2", mAD[(tt - 1, 3, 6 + (e % 2))])
                        if e >= 2:
                            p.wait(t, "sV2", mCB_v[(tt, e - 2)])
                        t.matmul(ps_x[e % 2][:], onesb[:], crw[tt % 2][0:1, e * 512:(e + 1) * 512],
                                 start=True, stop=True).then_inc(p.sem["sTm"], 1)
                    for e in range(NE):
                        for ht in range(32):
                            j = (tt * NE + e) * 8 + ht // 4
                            p.wait(t, WGSEM(j), mWG[(tt, e, ht // 4)][1])
                            if ht >= 1:
                                p.wait(t, "sP", mSI[(tt, e, ht - 1)])
                            elif (tt, e) != (0, 0):
                                pe = (tt * NE + e - 1)
                                p.wait(t, "sP", mSI[(pe // NE, pe % NE, 31)])
                            last = None
                            for dc in range(NC):
                                last = t.matmul(ps_g[:],
                                                wgs[j % 2][:, ((ht % 4) * 8 + dc) * 128:((ht % 4) * 8 + dc + 1) * 128],
                                                xTf[:, dc * S + tt * 512:dc * S + (tt + 1) * 512],
                                                start=(dc == 0), stop=(dc == NC - 1))
                            last.then_inc(p.sem["sTm"], 1)
                            if ht >= 1:
                                p.wait(t, "sVt", mH[(tt, e, ht - 1)])
                            elif (tt, e) != (0, 0):
                                pe = (tt * NE + e - 1)
                                p.wait(t, "sVt", mH[(pe // NE, pe % NE, 31)])
                            last = None
                            for dc in range(NC):
                                last = t.matmul(ps_u[:],
                                                wus[j % 2][:, ((ht % 4) * 8 + dc) * 128:((ht % 4) * 8 + dc + 1) * 128],
                                                xTf[:, dc * S + tt * 512:dc * S + (tt + 1) * 512],
                                                start=(dc == 0), stop=(dc == NC - 1))
                            last.then_inc(p.sem["sTm"], 1)
                        for half in range(2):
                            if (e, half) != (0, 0):
                                prev = (tt, e, half - 1) if half == 1 else (tt, e - 1, 1)
                            elif tt > 0:
                                prev = (tt - 1, NE - 1, 1)
                            else:
                                prev = None
                            if prev is not None:
                                p.wait(t, "sV2", mOE[prev])
                            for g4 in range(8):
                                j, wv = mWD[(tt, e, half, g4)]
                                p.wait(t, WDSEM(j), wv)
                                last = None
                                for hti in range(4):
                                    ht = g4 * 4 + hti
                                    if half == 0 and g4 == 0 and hti == 0:
                                        p.wait(t, "sVt", mH[(tt, e, 31)])
                                    for d in range(4):
                                        last = t.matmul(ps_o[d][:],
                                                        wds[j % 2][:, hti * 512 + d * 128:hti * 512 + (d + 1) * 128],
                                                        h_sb[:, ht * 512:(ht + 1) * 512],
                                                        start=(ht == 0), stop=(ht == 31))
                                last.then_inc(p.sem["sPV"], 1)
                    p.wait(t, "sV2", mOE[(tt, NE - 1, 1)])
                    p.wait(t, "sOE", OAC(tt))
                    for r in range(4):
                        for c in range(8):
                            u = (tt * 4 + r) * 8 + c
                            if u >= 2:
                                pu = u - 2
                                p.wait(t, "sV2", mAD[(pu // 32, (pu % 32) // 8, pu % 8)])
                            t.transpose(ps_x[c % 2][:, 0:128],
                                        oacc[:, c * 512 + r * 128:c * 512 + (r + 1) * 128],
                                        idf[:]).then_inc(p.sem["sTt"], 1)

            @block.vector
            def _(v):
                for tt in range(8):
                    for e in range(NE):
                        p.wait(v, "sTm", mCB_t[(tt, e)])
                        v.tensor_copy(cbce[:, e * 512:(e + 1) * 512], ps_x[e % 2][:]
                                      ).then_inc(p.sem["sV2"], 1)
                    prev_add = [0]
                    for e in range(NE):
                        for ht in range(32):
                            if ht == 0:
                                if (tt, e) != (0, 0):
                                    pe = tt * NE + e - 1
                                    p.wait(v, "sPV", mDG[(pe // NE, pe % NE, 1, 7)])
                            p.wait(v, "sP", mSI[(tt, e, ht)])
                            p.wait(v, "sTm", mGU[(tt, e, ht, 1)])
                            v.tensor_mul(h_sb[:, ht * 512:(ht + 1) * 512], sg[ht % 2][:],
                                         ps_u[:]).then_inc(p.sem["sVt"], 1)
                        for half in range(2):
                            p.wait(v, "sPV", mDG[(tt, e, half, 7)])
                            if e == 0 and half == 0 and tt > 0:
                                p.wait(v, "sTt", mTR[(tt - 1, 3, 7)])
                            if e == 0:
                                last = None
                                for d in range(4):
                                    last = v.tensor_mul(oacc[:, (half * 4 + d) * 512:(half * 4 + d + 1) * 512],
                                                        ps_o[d][:], cbce[:, e * 512:(e + 1) * 512])
                                last.then_inc(p.sem["sV2"], 1)
                            else:
                                if prev_add[0]:
                                    p.wait(v, "sVc", prev_add[0])
                                for d in range(4):
                                    ins = v.tensor_mul(otmp[:, d * 512:(d + 1) * 512],
                                                       ps_o[d][:], cbce[:, e * 512:(e + 1) * 512])
                                ins.then_inc(p.sem["sV2"], 1)
                                p.wait(v, "sV2", mOE[(tt, e, half)])
                                last = None
                                for d in range(4):
                                    last = v.tensor_add(oacc[:, (half * 4 + d) * 512:(half * 4 + d + 1) * 512],
                                                        oacc[:, (half * 4 + d) * 512:(half * 4 + d + 1) * 512],
                                                        otmp[:, d * 512:(d + 1) * 512])
                                if (e, half) == (NE - 1, 1):
                                    last.then_inc(p.sem["sOE"], 1)
                                    prev_add[0] = 0
                                else:
                                    prev_add[0] = p.inc(last, "sVc")
                    for r in range(4):
                        p.wait(v, "sX", mXR[(tt, r)])
                        if tt * 4 + r >= 1:
                            pr = divmod(tt * 4 + r - 1, 4)
                            p.wait(v, "sGD", mST[(pr[0], pr[1])])
                        last = None
                        for c in range(8):
                            p.wait(v, "sTt", mTR[(tt, r, c)])
                            last = v.tensor_add(xnw[:, c * 128:(c + 1) * 128], ps_x[c % 2][:, 0:128],
                                                xr[:, c * 128:(c + 1) * 128])
                            last.then_inc(p.sem["sV2"], 1)




def emit_s3(nc, p, src, dst, wqkv_bf, wout_bf, bqkv_pp_in, bv_bc_in, bout_bc_in,
            masks_in, q3_hbm, vpad_d, idb, idf, onesb, epsc, cast_all, gate):
    with contextlib.ExitStack() as st:
        kpad = st.enter_context(nc.sbuf_tensor("s3_kpad", [128, NC * 4112], BF16))
        woutT = st.enter_context(nc.sbuf_tensor("s3_woT", [128, NC * E], BF16))
        bpp = st.enter_context(nc.sbuf_tensor("s3_bpp", [128, 16], F32))
        bvb = st.enter_context(nc.sbuf_tensor("s3_bvb", [128, E], F32))
        bob = st.enter_context(nc.sbuf_tensor("s3_bob", [128, E], F32))
        msk1 = st.enter_context(nc.sbuf_tensor("s3_m1", [128, 3 * 128], BF16))
        msk2 = st.enter_context(nc.sbuf_tensor("s3_m2", [16, 3 * 128], BF16))

        stb = st.enter_context(contextlib.ExitStack())
        wqkvT = stb.enter_context(nc.sbuf_tensor("s3_wqT", [128, NC * 3072], BF16))
        with contextlib.ExitStack() as s2, nc.Block() as block:
            wtmp = [s2.enter_context(nc.sbuf_tensor(f"s3_wt{i}", [128, E], BF16)) for i in range(2)]
            ps_w = [s2.enter_context(nc.psum_tensor(f"s3_pw{i}", [128, 128], BF16)) for i in range(2)]

            bWp = (p.cnt["sW"], p.cnt["sW2"]); bTt = p.cnt["sTt"]; bVt = p.cnt["sVt"]
            WSEM = lambda ob: ("sW", "sW2")[ob % 2]
            W_ = lambda ob: bWp[ob % 2] + 16 * (ob // 2 + 1)
            Tt = lambda u: bTt + u + 1
            Vt = lambda u: bVt + u + 1
            p.bump("sW", 16 * 16); p.bump("sW2", 16 * 16)
            cst_mark = p.bump("sCR", 16 * 5)
            p.bump("sTt", 32 * 8); p.bump("sVt", 32 * 8)

            @block.sync
            def _(sync):
                sync.wait_ge(p.sem["sTG"], cast_all[0])
                sync.wait_ge(p.sem["sC2"], cast_all[1])
                for ob in range(32):
                    w_bf = wqkv_bf if ob < 24 else wout_bf
                    row = ob * 128 if ob < 24 else (ob - 24) * 128
                    if ob >= 2:
                        p.wait(sync, "sVt", Vt((ob - 1) * 8))
                    sync.dma_start(out=wtmp[ob % 2][:],
                                   in_=ap(w_bf, row * E, [[E, 128], [1, E]])).then_inc(p.sem[WSEM(ob)], 16)
                sync.dma_start(out=bpp[:], in_=bqkv_pp_in).then_inc(p.sem["sCR"], 16)
                sync.dma_start(out=bvb[:], in_=bv_bc_in).then_inc(p.sem["sCR"], 16)
                sync.dma_start(out=bob[:], in_=bout_bc_in).then_inc(p.sem["sCR"], 16)
                sync.dma_start(out=msk1[:], in_=ap(masks_in, 0, [[128, 128], [144 * 128, 3], [1, 128]])
                               ).then_inc(p.sem["sCR"], 16)
                sync.dma_start(out=msk2[:], in_=ap(masks_in, 128 * 128, [[128, 16], [144 * 128, 3], [1, 128]])
                               ).then_inc(p.sem["sCR"], 16)

            @block.tensor
            def _(t):
                for ob in range(32):
                    p.wait(t, WSEM(ob), W_(ob))
                    for ic in range(8):
                        u = ob * 8 + ic
                        if u >= 2:
                            p.wait(t, "sVt", Vt(u - 2))
                        t.transpose(ps_w[u % 2][:], wtmp[ob % 2][:, ic * 128:(ic + 1) * 128],
                                    idb[:]).then_inc(p.sem["sTt"], 1)

            @block.vector
            def _(v):
                v.memset(kpad[:], 0.0)
                for ob in range(32):
                    for ic in range(8):
                        u = ob * 8 + ic
                        p.wait(v, "sTt", Tt(u))
                        if ob < 24:
                            dst_ap = wqkvT[:, ic * 3072 + ob * 128: ic * 3072 + (ob + 1) * 128]
                        else:
                            dst_ap = woutT[:, ic * E + (ob - 24) * 128: ic * E + (ob - 23) * 128]
                        v.tensor_copy(dst_ap, ps_w[u % 2][:]).then_inc(p.sem["sVt"], 1)

        with contextlib.ExitStack() as s2, nc.Block() as block:
            xt = [s2.enter_context(nc.sbuf_tensor(f"s3_xt{i}", [128, E], F32)) for i in range(2)]
            xcb = [s2.enter_context(nc.sbuf_tensor(f"s3_xc{i}", [128, E], F32)) for i in range(2)]
            xh = [s2.enter_context(nc.sbuf_tensor(f"s3_xh{i}", [128, E], BF16)) for i in range(2)]
            xT = [s2.enter_context(nc.sbuf_tensor(f"s3_xT{i}", [128, E], BF16)) for i in range(2)]
            qtmp = [s2.enter_context(nc.sbuf_tensor(f"s3_qt{i}", [128, NC * 128], BF16)) for i in range(2)]
            vtmp = [s2.enter_context(nc.sbuf_tensor(f"s3_vt{i}", [128, H * 65], BF16)) for i in range(2)]
            sqd = s2.enter_context(nc.sbuf_tensor("s3_sqd", [128, E], F32))
            sml = s2.enter_context(nc.sbuf_tensor("s3_sm", [128, 6], F32))
            ztile = s2.enter_context(nc.sbuf_tensor("s3_z", [128, H * 65], BF16))
            ps_tr = [s2.enter_context(nc.psum_tensor(f"s3_ptr{i}", [128, 128], BF16)) for i in range(2)]
            ps_q = [s2.enter_context(nc.psum_tensor(f"s3_pq{i}", [128, 512], F32)) for i in range(2)]
            ps_k = [s2.enter_context(nc.psum_tensor(f"s3_pk{i}", [128, 512], F32)) for i in range(2)]
            ps_v = [s2.enter_context(nc.psum_tensor(f"s3_pv{i}", [128, 512], F32)) for i in range(2)]

            bXp = (p.cnt["sX"], p.cnt["sX2"]); bV = p.cnt["sV"]; bS = p.cnt["sS"]
            bTt = p.cnt["sTt"]; bVt = p.cnt["sVt"]; bTm = p.cnt["sTm"]
            bE = p.cnt["sV2"]; bKQp = (p.cnt["sKQ"], p.cnt["sKQ2"])
            bDp = (p.cnt["sD"], p.cnt["sD2"])
            XSEM = lambda b: ("sX", "sX2")[b % 2]
            X_ = lambda b: bXp[b % 2] + 16 * (b // 2 + 1)
            V_ = lambda b, i: bV + 2 * b + i
            S_ = lambda b, i: bS + 2 * b + i
            Tt = lambda g: bTt + g + 1
            Vt = lambda g: bVt + g + 1
            Tm = lambda b, i: bTm + 3 * b + i
            E_ = lambda b, i: bE + 3 * b + i
            KQSEM = lambda b: ("sKQ", "sKQ2")[b % 2]
            KQ = lambda b: bKQp[b % 2] + 16 * (b // 2 + 1)
            DSEM = lambda b: ("sD", "sD2")[b % 2]
            D_ = lambda b: bDp[b % 2] + 16 * (b // 2 + 1)
            p.bump("sX", 16 * (NBLK // 2)); p.bump("sX2", 16 * (NBLK // 2))
            p.bump("sV", 2 * NBLK); p.bump("sS", 2 * NBLK)
            p.bump("sTt", 8 * NBLK); p.bump("sVt", 8 * NBLK); p.bump("sTm", 3 * NBLK)
            p.bump("sV2", 3 * NBLK)
            p.bump("sKQ", 16 * (NBLK // 2)); p.bump("sKQ2", 16 * (NBLK // 2))
            p.bump("sD", 16 * (NBLK // 2)); p.bump("sD2", 16 * (NBLK // 2))
            zt_mark = p.bump("sLx", 1)

            @block.gpsimd
            def _(g):
                p.wait(g, "sD", gate[0])
                p.wait(g, "sD2", gate[1])
                p.wait(g, "sGD", gate[2])
                for b in range(NBLK):
                    if b >= 2:
                        p.wait(g, "sS", S_(b - 2, 1))
                    g.dma_start(out=xt[b % 2][:],
                                in_=ap(src, b * 128 * E, [[E, 128], [1, E]])).then_inc(p.sem[XSEM(b)], 16)

            def s3_kqv_evacs(v, b):
                bb = b % 2
                p.wait(v, "sTm", Tm(b, 1))
                if b >= 2:
                    p.wait(v, KQSEM(b - 2), KQ(b - 2))
                last = None
                for bank in range(2):
                    for hp4 in range(4):
                        hp = bank * 4 + hp4
                        last = v.tensor_scalar(qtmp[bb][:, hp * 128:(hp + 1) * 128],
                                               ps_q[bank][:, hp4 * 128:(hp4 + 1) * 128],
                                               bpp[:, hp:hp + 1], None, ALU.add)
                last.then_inc(p.sem["sV2"], 1)
                p.wait(v, "sTm", Tm(b, 2))
                for bank in range(2):
                    for hp4 in range(4):
                        hp = bank * 4 + hp4
                        last = v.tensor_scalar(
                            kpad[:, hp * 4112 + 8 + b * 128: hp * 4112 + 8 + (b + 1) * 128],
                            ps_k[bank][:, hp4 * 128:(hp4 + 1) * 128],
                            bpp[:, 8 + hp:9 + hp], None, ALU.add)
                last.then_inc(p.sem["sV2"], 1)
                p.wait(v, "sTm", Tm(b, 3))
                if b >= 2:
                    p.wait(v, DSEM(b - 2), D_(b - 2))
                for half in range(2):
                    dst_ap = ap(vtmp[bb], half * 8 * 65, [[H * 65, 128], [65, 8], [1, 64]])
                    last = v.scalar_tensor_tensor(dst_ap, ps_v[half][:], 1.0,
                                                  bvb[:, half * 512:(half + 1) * 512],
                                                  ALU.mult, ALU.add)
                last.then_inc(p.sem["sV2"], 1)

            @block.vector
            def _(v):
                    p.wait(v, "sCR", cst_mark)
                    v.memset(vtmp[0][:], 1.0)
                    v.memset(vtmp[1][:], 1.0)
                    v.memset(ztile[:], 0.0).then_inc(p.sem["sLx"], 1)
                    for b in range(NBLK):
                        bb = b % 2
                        p.wait(v, XSEM(b), X_(b))
                        if b >= 2:
                            p.wait(v, "sS", S_(b - 2, 1))
                        ln_vec_pre(p, v, xt[bb][:], sml[:, bb:bb + 1], xcb[bb][:], "sV")
                        p.wait(v, "sS", S_(b, 2))
                        if b >= 2:
                            p.wait(v, "sTt", Tt((b - 2) * 8 + 7))
                        v.tensor_scalar(xh[bb][:], xcb[bb][:], sml[:, 4 + bb:5 + bb], None,
                                        ALU.mult).then_inc(p.sem["sV"], 1)
                        for c in range(NC):
                            g = b * 8 + c
                            p.wait(v, "sTt", Tt(g))
                            v.tensor_copy(xT[bb][:, c * 128:(c + 1) * 128],
                                          ps_tr[c % 2][:]).then_inc(p.sem["sVt"], 1)
                        if b >= 1:
                            s3_kqv_evacs(v, b - 1)
                    s3_kqv_evacs(v, NBLK - 1)

            @block.scalar
            def _(sc):
                for b in range(NBLK):
                    bb = b % 2
                    p.wait(sc, "sV", V_(b, 1))
                    if b >= 2:
                        p.wait(sc, "sV", V_(b - 2, 2))
                    ln_scalar(p, sc, xcb[bb][:], sqd[:], sml[:, 2 + bb:3 + bb],
                              sml[:, 4 + bb:5 + bb], epsc[:, 0:1], "sS", S_(b, 1))

            @block.tensor
            def _(t):
                for b in range(NBLK):
                    bb = b % 2
                    p.wait(t, "sV", V_(b, 2))
                    for c in range(NC):
                        g = b * 8 + c
                        if g >= 2:
                            p.wait(t, "sVt", Vt(g - 2))
                        t.transpose(ps_tr[c % 2][:], xh[bb][:, c * 128:(c + 1) * 128],
                                    idb[:]).then_inc(p.sem["sTt"], 1)
                    p.wait(t, "sVt", Vt(b * 8 + 7))
                    if b >= 1:
                        p.wait(t, "sV2", E_(b - 1, 3))
                    for qk, psb in ((0, ps_q), (1, ps_k)):
                        last = None
                        for bank in range(2):
                            for hp4 in range(4):
                                hp = bank * 4 + hp4
                                for c in range(NC):
                                    last = t.matmul(
                                        psb[bank][:, hp4 * 128:(hp4 + 1) * 128],
                                        wqkvT[:, c * 3072 + qk * 1024 + hp * 128:
                                              c * 3072 + qk * 1024 + (hp + 1) * 128],
                                        xT[bb][:, c * 128:(c + 1) * 128],
                                        start=(hp4 == 0 and c == 0), stop=(hp4 == 3 and c == NC - 1))
                        last.then_inc(p.sem["sTm"], 1)
                    last = None
                    for half in range(2):
                        for c in range(NC):
                            last = t.matmul(ps_v[half][:],
                                            xT[bb][:, c * 128:(c + 1) * 128],
                                            wqkvT[:, c * 3072 + 2048 + half * 512:
                                                  c * 3072 + 2048 + (half + 1) * 512],
                                            start=(c == 0), stop=(c == NC - 1))
                    last.then_inc(p.sem["sTm"], 1)


            @block.sync
            def _(sync):
                for b in range(NBLK):
                    p.wait(sync, "sV2", E_(b, 1))
                    sync.dma_start(out=ap(q3_hbm, b * 128, [[S, 128], [128 * S, NC], [1, 128]]),
                                   in_=qtmp[b % 2][:]).then_inc(p.sem[KQSEM(b)], 16)
                    p.wait(sync, "sV2", E_(b, 3))
                    sync.dma_start(out=ap(vpad_d, (8 + b * 128) * (H * 65), [[H * 65, 128], [1, H * 65]]),
                                   in_=vtmp[b % 2][:]).then_inc(p.sem[DSEM(b)], 16)
                sync.wait_ge(p.sem["sLx"], zt_mark)
                sync.dma_start(out=ap(vpad_d, 0, [[H * 65, 8], [1, H * 65]]),
                               in_=ztile[0:8, :]).then_inc(p.sem["sCR"], 16)
                sync.dma_start(out=ap(vpad_d, 4104 * (H * 65), [[H * 65, 120], [1, H * 65]]),
                               in_=ztile[0:120, :]).then_inc(p.sem["sCR"], 16)
            p.bump("sCR", 32)
            vpd_done = (p.cnt["sD"], p.cnt["sD2"], p.cnt["sKQ"], p.cnt["sKQ2"], p.cnt["sCR"])

        stb.close()
        with contextlib.ExitStack() as s2, nc.Block() as block:
            vpad = s2.enter_context(nc.sbuf_tensor("s3_vpad", [128, 33 * (H * 65)], BF16))
            qts = [s2.enter_context(nc.sbuf_tensor(f"s3_qts{i}", [128, NC * 128], BF16)) for i in range(2)]
            pt1 = [s2.enter_context(nc.sbuf_tensor(f"s3_pt1{i}", [128, 128], BF16)) for i in range(2)]
            pt2 = [s2.enter_context(nc.sbuf_tensor(f"s3_pt2{i}", [16, 128], BF16)) for i in range(2)]
            rsb = s2.enter_context(nc.sbuf_tensor("s3_rsb", [1, H * 128], F32))
            rcb = s2.enter_context(nc.sbuf_tensor("s3_rcb", [1, H * 128], BF16))
            rsT = s2.enter_context(nc.sbuf_tensor("s3_rsT", [128, 16], F32))
            rsTb = s2.enter_context(nc.sbuf_tensor("s3_rsTb", [128, 16], BF16))
            rcs = s2.enter_context(nc.sbuf_tensor("s3_rcs", [64, H * 128], F32))
            ctxT = s2.enter_context(nc.sbuf_tensor("s3_ctxT", [128, NC * 128], BF16))
            yb = s2.enter_context(nc.sbuf_tensor("s3_yb", [128, E], F32))
            xr = s2.enter_context(nc.sbuf_tensor("s3_xr", [128, E], F32))
            xnw = s2.enter_context(nc.sbuf_tensor("s3_xnw", [128, E], F32))
            ps_sA = [s2.enter_context(nc.psum_tensor(f"s3_psA{i}", [128, 512], F32)) for i in range(2)]
            ps_ob = [s2.enter_context(nc.psum_tensor(f"s3_pob{i}", [65, 512], F32)) for i in range(4)]
            ps_y = [s2.enter_context(nc.psum_tensor(f"s3_py{i}", [128, 512], F32)) for i in range(2)]

            bKQp = (p.cnt["sKQ"], p.cnt["sKQ2"])
            bST = p.cnt["sST"]; bP = p.cnt["sP"]; bPV = p.cnt["sPV"]
            bOE = p.cnt["sOE"]; bTm = p.cnt["sTm"]; bE = p.cnt["sV2"]; bX = p.cnt["sX"]
            bD = p.cnt["sGD"]; bV = p.cnt["sV"]
            KQSEM = lambda qt: ("sKQ", "sKQ2")[qt % 2]
            KQ = lambda qt: bKQp[qt % 2] + 16 * (qt // 2 + 1)
            ST = lambda n: bST + n + 1
            p.bump("sKQ", 16 * 16); p.bump("sKQ2", 16 * 16)
            p.bump("sST", 32 * 16); p.bump("sP", 32 * 16)
            p.bump("sPV", 32 * 16)
            MSK = lambda n: bOE + n + 1
            p.bump("sOE", 16 * 32)
            bZB = p.cnt["sZB"]
            RSB = lambda qt: bZB + 2 * qt + 1
            RCD = lambda qt: bZB + 2 * qt + 2
            p.bump("sZB", 64)
            v2c = bE
            mRC = {}; mCTX = {}
            for qt in range(32):
                for j in range(4):
                    mRC[(qt, j)] = v2c = v2c + 1
                for hh in range(16):
                    mCTX[(qt, hh)] = v2c = v2c + 1
            p.bump("sV2", v2c - bE)
            EXP = lambda n: bP + n + 1
            PVm = lambda n: bPV + n + 1
            tmc = bTm
            mRCT = {}; mY = {}
            for qt in range(32):
                for j in range(4):
                    mRCT[(qt, j)] = tmc = tmc + 1
                for nh in range(2):
                    mY[(qt, nh)] = tmc = tmc + 1
            p.bump("sTm", tmc - bTm)
            XR = lambda qt: bX + 16 * (qt + 1)
            D_ = lambda qt: bD + 16 * (qt + 1)
            XN = lambda qt: bV + qt + 1
            p.bump("sX", 16 * 32); p.bump("sGD", 16 * 32); p.bump("sV", 32)
            vp_mark = p.bump("sW", 16)
            bCR3 = p.cnt["sCR"]
            SCT = lambda qt: bCR3 + 32 * qt + 16
            GTH = lambda qt: bCR3 + 32 * qt + 32
            p.bump("sCR", 32 * 32)

            @block.sync
            def _(sync):
                sync.wait_ge(p.sem["sD"], vpd_done[0])
                sync.wait_ge(p.sem["sD2"], vpd_done[1])
                sync.wait_ge(p.sem["sKQ"], vpd_done[2])
                sync.wait_ge(p.sem["sKQ2"], vpd_done[3])
                sync.wait_ge(p.sem["sCR"], vpd_done[4])
                sync.dma_start(out=vpad[:],
                               in_=ap(vpad_d, 0, [[H * 65, 128], [128 * H * 65, 33], [1, H * 65]])
                               ).then_inc(p.sem["sW"], 16)
                for qt in range(32):
                    if qt >= 2:
                        p.wait(sync, "sST", ST(16 * (qt - 1) - 1))
                    sync.dma_start(out=qts[qt % 2][:],
                                   in_=ap(q3_hbm, qt * 128, [[S, 128], [128 * S, NC], [1, 128]])
                                   ).then_inc(p.sem[KQSEM(qt)], 16)
                    p.wait(sync, "sZB", RSB(qt))
                    sync.dma_start(out=rsT[:], in_=rsb[0:1, :]).then_inc(p.sem["sCR"], 16)
                    p.wait(sync, "sZB", RCD(qt))
                    sync.dma_start(out=rcb[0:1, :], in_=rsTb[:]).then_inc(p.sem["sCR"], 16)

            @block.gpsimd
            def _(g):
                p.wait(g, "sD", gate[0])
                p.wait(g, "sD2", gate[1])
                p.wait(g, "sGD", gate[2])
                for qt in range(32):
                    if qt >= 1:
                        p.wait(g, "sV", XN(qt - 1))
                        g.dma_start(out=ap(dst, (qt - 1) * 128 * E, [[E, 128], [1, E]]),
                                    in_=xnw[:]).then_inc(p.sem["sGD"], 16)
                    g.dma_start(out=xr[:],
                                in_=ap(src, qt * 128 * E, [[E, 128], [1, E]])).then_inc(p.sem["sX"], 16)
                p.wait(g, "sV", XN(31))
                g.dma_start(out=ap(dst, 31 * 128 * E, [[E, 128], [1, E]]),
                            in_=xnw[:]).then_inc(p.sem["sGD"], 16)

            @block.tensor
            def _(t):
                t.wait_ge(p.sem["sW"], vp_mark)
                for qt in range(32):
                    p.wait(t, KQSEM(qt), KQ(qt))

                    def s3_s(hh):
                        n = qt * 16 + hh
                        hp = hh // 2; hq = (hh % 2) * 64
                        if n >= 2:
                            p.wait(t, "sP", EXP(n - 2))
                        t.matmul(ps_sA[n % 2][:, 0:128],
                                 kpad[0:64, hp * 4112 + qt * 128: hp * 4112 + (qt + 1) * 128]
                                 if hq == 0 else
                                 kpad[64:128, hp * 4112 + qt * 128: hp * 4112 + (qt + 1) * 128],
                                 qts[qt % 2][hq:hq + 64, hp * 128:(hp + 1) * 128],
                                 start=True, stop=True)
                        t.matmul(ps_sA[n % 2][0:16, 128:256],
                                 kpad[0:64, hp * 4112 + (qt + 1) * 128: hp * 4112 + (qt + 1) * 128 + 16]
                                 if hq == 0 else
                                 kpad[64:128, hp * 4112 + (qt + 1) * 128: hp * 4112 + (qt + 1) * 128 + 16],
                                 qts[qt % 2][hq:hq + 64, hp * 128:(hp + 1) * 128],
                                 start=False, stop=True,
                                 skip_group_check=True).then_inc(p.sem["sST"], 1)

                    def s3_pv(hh):
                        n = qt * 16 + hh
                        if hh % 4 == 0 and qt > 0:
                            p.wait(t, "sV2", mCTX[(qt - 1, 15)])
                        p.wait(t, "sOE", MSK(n))
                        t.matmul(ps_ob[hh // 4][:, (hh % 4) * 128:(hh % 4 + 1) * 128],
                                 vpad[:, qt * (H * 65) + hh * 65: qt * (H * 65) + hh * 65 + 65],
                                 pt1[n % 2][:],
                                 start=(hh % 4 == 0), stop=True, skip_group_check=True)
                        t.matmul(ps_ob[hh // 4][:, (hh % 4) * 128:(hh % 4 + 1) * 128],
                                 vpad[0:16, (qt + 1) * (H * 65) + hh * 65: (qt + 1) * (H * 65) + hh * 65 + 65],
                                 pt2[n % 2][:],
                                 start=False, stop=True,
                                 skip_group_check=True).then_inc(p.sem["sPV"], 1)

                    for hh in range(16):
                        s3_s(hh)
                        if hh >= 1:
                            s3_pv(hh - 1)
                    s3_pv(15)
                    p.wait(t, "sCR", GTH(qt))
                    for j in range(4):
                        if j < 2:
                            if qt > 0:
                                p.wait(t, "sV", XN(qt - 1))
                        else:
                            p.wait(t, "sV2", mRC[(qt, j - 2)])
                        t.matmul(ps_y[j % 2][0:64, :], onesb[0:1, 0:64], rcb[0:1, j * 512:(j + 1) * 512],
                                 start=True, stop=True).then_inc(p.sem["sTm"], 1)
                    p.wait(t, "sV2", mCTX[(qt, 15)])
                    for nh in range(2):
                        p.wait(t, "sV2", mRC[(qt, 2 + nh)])
                        last = None
                        for cc in range(8):
                            last = t.matmul(ps_y[nh][:],
                                            ctxT[:, cc * 128:(cc + 1) * 128],
                                            woutT[:, cc * E + nh * 512: cc * E + (nh + 1) * 512],
                                            start=(cc == 0), stop=(cc == 7))
                        last.then_inc(p.sem["sTm"], 1)

            @block.scalar
            def _(sc):
                for qt in range(32):
                    for hh in range(16):
                        n = qt * 16 + hh
                        p.wait(sc, "sST", ST(n))
                        if n >= 2:
                            p.wait(sc, "sPV", PVm(n - 2))
                        sc.activation(pt1[n % 2][:], ps_sA[n % 2][:, 0:128], AF.Exp,
                                      scale=1.0 / 8.0)
                        sc.activation(pt2[n % 2][:], ps_sA[n % 2][0:16, 128:256], AF.Exp,
                                      scale=1.0 / 8.0).then_inc(p.sem["sP"], 1)

            @block.vector
            def _(v):
                for qt in range(32):
                    cls = 0 if qt == 0 else (2 if qt == 31 else 1)
                    for hh in range(16):
                        n = qt * 16 + hh
                        p.wait(v, "sP", EXP(n))
                        v.tensor_mul(pt1[n % 2][:], pt1[n % 2][:],
                                     msk1[:, cls * 128:(cls + 1) * 128])
                        v.tensor_mul(pt2[n % 2][:], pt2[n % 2][:],
                                     msk2[:, cls * 128:(cls + 1) * 128]).then_inc(p.sem["sOE"], 1)
                    if qt >= 1:
                        p.wait(v, "sCR", SCT(qt - 1))
                    p.wait(v, "sPV", PVm(qt * 16 + 15))
                    for b4 in range(4):
                        ins = v.tensor_copy(rsb[:, b4 * 512:(b4 + 1) * 512],
                                            ps_ob[b4][64:65, :])
                    ins.then_inc(p.sem["sZB"], 1)
                    p.wait(v, "sCR", SCT(qt))
                    cc = p.inc(v.reciprocal(rsT[:], rsT[:]), "sVc")
                    p.wait(v, "sVc", cc)
                    v.tensor_copy(rsTb[:], rsT[:]).then_inc(p.sem["sZB"], 1)
                    for j in range(4):
                        p.wait(v, "sTm", mRCT[(qt, j)])
                        v.tensor_copy(rcs[:, j * 512:(j + 1) * 512],
                                      ps_y[j % 2][0:64, :]).then_inc(p.sem["sV2"], 1)
                    p.wait(v, "sV2", mRC[(qt, 3)])
                    for hh in range(16):
                        hp = hh // 2; hq = (hh % 2) * 64
                        v.tensor_mul(ctxT[hq:hq + 64, hp * 128:(hp + 1) * 128],
                                     ps_ob[hh // 4][0:64, (hh % 4) * 128:(hh % 4 + 1) * 128],
                                     rcs[:, hh * 128:(hh + 1) * 128]).then_inc(p.sem["sV2"], 1)
                    p.wait(v, "sX", XR(qt))
                    if qt >= 1:
                        p.wait(v, "sGD", D_(qt - 1))
                    for nh in range(2):
                        p.wait(v, "sTm", mY[(qt, nh)])
                        cc = p.inc(v.tensor_add(yb[:, nh * 512:(nh + 1) * 512], ps_y[nh][:],
                                                bob[:, nh * 512:(nh + 1) * 512]), "sVc")
                    p.wait(v, "sVc", cc)
                    v.tensor_add(xnw[:], yb[:], xr[:]).then_inc(p.sem["sV"], 1)




def build(nc):
    def inp(name, shape, dt=F32):
        return nc.dram_tensor(name, list(shape), dt, kind="ExternalInput").ap()

    x_in = inp("x", [S, E])
    wq_in = inp("wq", [E, E]); wk_in = inp("wk", [E, E]); wv_in = inp("wv", [E, E])
    tg_in = inp("tg", [8, 256, S])
    gamma_bc = inp("gamma_bc", [128, E]); beta_bc = inp("beta_bc", [128, E])
    wqkv_in = inp("wqkv", [3 * E, E]); bqkv_pp = inp("bqkv_pp", [128, 16]); bv_bc = inp("bv_bc", [128, E])
    wout_in = inp("wout", [E, E]); bout_bc = inp("bout_bc", [128, E])
    wr_in = inp("wr", [E, NE])
    wg_in = inp("wg", [NE * E * HID]); wu_in = inp("wu", [NE * E * HID])
    wd_in = inp("wdn", [NE, HID, E])
    masks_in = inp("masks", [3, 144, 128], BF16)
    idf_in = inp("idf", [128, 128]); idb_in = inp("idb", [128, 128], BF16)
    out_t = nc.dram_tensor("out", [S, E], F32, kind="ExternalOutput").ap()

    def scratch(name, shape, dt):
        return nc.dram_tensor(name, list(shape), dt, kind="Internal").ap()

    resid = scratch("resid", [S, E], F32)
    wq_bf = scratch("wq_bf", [E, E], BF16)
    wk_bf = scratch("wk_bf", [E, E], BF16)
    wv_bf = scratch("wv_bf", [E, E], BF16)
    tg_bf = scratch("tg_bf", [8 * 256 * S], BF16)
    wgu_bf = scratch("wgu_bf", [2 * NE * 32 * 128 * 8 * 128], BF16)
    wd_bf = scratch("wd_bf", [NE * HID * E], BF16)
    wqkv_bf = scratch("wqkv_bf", [3 * E, E], BF16)
    wout_bf = scratch("wout_bf", [E, E], BF16)
    k1_hbm = scratch("k1_hbm", [NC * 128 * S], BF16)
    q3_hbm = scratch("q3_hbm", [NC * 128 * S], BF16)
    vpad_d = scratch("vpad_d", [33 * 128 * H * 65], BF16)
    ct_dram = scratch("ct_dram", [NE, S], BF16)

    p = P(nc)
    p.mksem("sC", "sX", "sW", "sKQ", "sTt", "sVt", "sTm", "sV2", "sV", "sS",
            "sP", "sPV", "sST", "sOE", "sLx", "sLr", "sVc", "sSc", "sD",
            "sC2", "sX2", "sW2", "sKQ2", "sD2", "sCR", "sCR2", "sGD", "sTG", "sZB")

    with p.ctx, contextlib.ExitStack() as stg:
        idb = stg.enter_context(nc.sbuf_tensor("g_idb", [128, 128], BF16))
        idf = stg.enter_context(nc.sbuf_tensor("g_idf", [128, 128], F32))
        onesb = stg.enter_context(nc.sbuf_tensor("g_ones", [1, 128], BF16))
        epsc = stg.enter_context(nc.sbuf_tensor("g_eps", [128, 2], F32))

        with nc.Block() as block:
            @block.sync
            def _(sync):
                sync.dma_start(out=idb[:], in_=idb_in).then_inc(p.sem["sW"], 16)
                sync.dma_start(out=idf[:], in_=idf_in).then_inc(p.sem["sW"], 16)
            p.bump("sW", 32)
            p.aux_constw = p.cnt["sW"]

            @block.vector
            def _(v):
                v.memset(onesb[:], 1.0)
                v.memset(epsc[:, 0:1], 1e-5)
                v.memset(epsc[:, 1:2], 1e-3)

            @block.gpsimd
            def _(g):
                p.inc(g.dma_start(out=ap(tg_bf, 0, [[1, 8 * 256 * S]]),
                                  in_=ap(tg_in, 0, [[1, 8 * 256 * S]])), "sTG", 16)
                for mi, w_src in ((0, wg_in), (1, wu_in)):
                    for e in range(NE):
                        src = ap(w_src, e * E * HID, [[1, E * HID]])
                        dst = ap(wgu_bf, (mi * NE + e) * E * HID, [[1, E * HID]])
                        p.inc(g.dma_start(out=dst, in_=src), "sC2", 16)
                for e in range(NE):
                    for qtr in range(4):
                        p.inc(g.dma_start(out=ap(wd_bf, (e * HID + qtr * 1024) * E, [[1, 1024 * E]]),
                                          in_=ap(wd_in, (e * HID + qtr * 1024) * E, [[1, 1024 * E]])),
                              "sC2", 16)
                p.inc(g.dma_start(out=wqkv_bf, in_=wqkv_in), "sC2", 16)
                p.inc(g.dma_start(out=wout_bf, in_=wout_in), "sC2", 16)
        cast1 = p.cnt["sTG"]
        cast_all = (cast1, p.cnt["sC2"])

        s1_dst = resid if N_PHASES > 1 else out_t
        emit_s1(nc, p, x_in, s1_dst, wq_in, wk_in, wv_in, tg_bf, gamma_bc, beta_bc,
                k1_hbm, idb, idf, onesb, epsc, cast1)
        g1 = (p.cnt["sD"], p.cnt["sD2"], p.cnt["sGD"])
        if N_PHASES >= 2:
            s2_dst = resid if N_PHASES > 2 else out_t
            emit_moe(nc, p, resid, s2_dst, wgu_bf, wd_bf, wr_in, ct_dram, idb, idf, onesb, epsc, cast_all, "m1", g1)
        g2 = (p.cnt["sD"], p.cnt["sD2"], p.cnt["sGD"])
        if N_PHASES >= 3:
            s3_dst = resid if N_PHASES > 3 else out_t
            emit_s3(nc, p, resid, s3_dst, wqkv_bf, wout_bf, bqkv_pp, bv_bc, bout_bc,
                    masks_in, q3_hbm, vpad_d, idb, idf, onesb, epsc, cast_all, g2)
        g3 = (p.cnt["sD"], p.cnt["sD2"], p.cnt["sGD"])
        if N_PHASES >= 4:
            emit_moe(nc, p, resid, out_t, wgu_bf, wd_bf, wr_in, ct_dram, idb, idf, onesb, epsc, cast_all, "m2", g3)

        with nc.Block() as block:
            fin = p.cnt["sD"]
            fin2 = p.cnt["sD2"]
            fin3 = p.cnt["sGD"]

            @block.sync
            def _(sync):
                sync.wait_ge(p.sem["sD"], fin)
                sync.wait_ge(p.sem["sD2"], fin2)
                sync.wait_ge(p.sem["sGD"], fin3)
    return nc


def _host_inputs(inputs):
    import ml_dtypes
    rel = np.asarray(inputs["L0_attn_a_rel_table"], np.float32)
    qd = np.arange(256)[:, None]
    jp = np.arange(256)[None, :]
    tgs = []
    for g in range(8, 16):
        rows = np.minimum(4095 - 256 * g - qd + jp, 2046)
        tgs.append(rel[rows].reshape(256, S))
    tg = np.ascontiguousarray(np.stack(tgs)).astype(np.float32)

    bqkv = np.asarray(inputs["L0_attn_b_b_qkv"], np.float32)
    import ml_dtypes
    f = np.arange(144)[:, None]
    pcol = np.arange(128)[None, :]
    band = (f - pcol >= 0) & (f - pcol <= 16)
    m_mid = np.where(band, 1.0, 0.0).astype(np.float32)
    m_first = np.where(band & (f >= 8), 1.0, 0.0).astype(np.float32)
    m_last = np.where(band & (f <= 135), 1.0, 0.0).astype(np.float32)
    masks = np.stack([m_first, m_mid, m_last]).astype(ml_dtypes.bfloat16)

    def bc(v):
        return np.ascontiguousarray(np.broadcast_to(np.asarray(v, np.float32), (128, E)))

    common = {
        "wq": np.asarray(inputs["L0_attn_a_Wq"], np.float32),
        "wk": np.asarray(inputs["L0_attn_a_Wk"], np.float32),
        "wv": np.asarray(inputs["L0_attn_a_Wv"], np.float32),
        "tg": tg,
        "gamma_bc": bc(inputs["L0_attn_a_gamma"]),
        "beta_bc": bc(inputs["L0_attn_a_beta"]),
        "wqkv": np.asarray(inputs["L0_attn_b_w_qkv"], np.float32),
        "bqkv_pp": np.ascontiguousarray(bqkv[0:2048].reshape(16, 128).T),
        "bv_bc": bc(bqkv[2048:3072]),
        "wout": np.asarray(inputs["L0_attn_b_w_out"], np.float32),
        "bout_bc": bc(inputs["L0_attn_b_b_out"]),
        "wr": np.asarray(inputs["L0_ffn_Wr"], np.float32),
        "wg": np.ascontiguousarray(
            np.asarray(inputs["L0_ffn_Wg"], np.float32)
            .reshape(NE, 8, 128, 32, 128).transpose(0, 3, 2, 1, 4)).ravel(),
        "wu": np.ascontiguousarray(
            np.asarray(inputs["L0_ffn_Wu"], np.float32)
            .reshape(NE, 8, 128, 32, 128).transpose(0, 3, 2, 1, 4)).ravel(),
        "wdn": np.asarray(inputs["L0_ffn_Wd"], np.float32),
        "masks": masks,
        "idf": np.eye(128, dtype=np.float32),
        "idb": np.eye(128, dtype=np.float32).astype(ml_dtypes.bfloat16),
    }
    x = np.asarray(inputs["L0_attn_a_x"], np.float32)
    in_maps = []
    for c in range(8):
        m = dict(common)
        m["x"] = np.ascontiguousarray(x[c])
        in_maps.append(m)
    return in_maps


def kernel(**inputs):
    in_maps = _host_inputs(inputs)
    nc = bass.Bass("TRN2", target_bir_lowering=False)
    build(nc)
    res = run_bass_kernel_spmd(nc, in_maps, core_ids=list(range(8)))
    return np.stack([np.asarray(r["out"], np.float32) for r in res.results])
